# version 14
# baseline (speedup 1.0000x reference)
import numpy as np
import concourse.bass as bass
import concourse.mybir as mybir
from concourse.bass_utils import run_bass_kernel_spmd

F32 = mybir.dt.float32
BF16 = mybir.dt.bfloat16
U8 = mybir.dt.uint8
AF = mybir.ActivationFunctionType
ALU = mybir.AluOpType
AXL = mybir.AxisListType

NCORE = 8
L = 2048
D = 1024
DI = 2048
NS = 16
RK = 64
FF = 4096
RPC = 256
NDSL = 16
NDT = 8
NFT = 32
NTOK = 16
LC = 1024
NCH = 2
EPS = 1e-5
KB = 1024


class Sched:
    ENGS = ("sp", "act", "dve", "pool", "te")
    KD = 12

    def __init__(self):
        self.phases = []
        self.cur = None
        self.xcnt = {e: 0 for e in self.ENGS}
        self.dcnt = {(e, j): 0 for e in self.ENGS for j in range(self.KD)}
        self.dseq = {e: 0 for e in self.ENGS}
        self.ccnt = 0

    def phase(self, name):
        self.cur = {e: [] for e in self.ENGS}
        self.cur["_name"] = name
        self.phases.append(self.cur)

    def op(self, eng, fn, deps=()):
        self.xcnt[eng] += 1
        self.cur[eng].append(("x", fn, tuple(deps), None))
        return (eng, "x", self.xcnt[eng])

    def dma(self, eng, out, in_, deps=()):
        j = self.dseq[eng] % self.KD
        self.dseq[eng] += 1
        self.dcnt[(eng, j)] += 1
        fn = lambda e, o=out, i=in_: e.dma_start(out=o, in_=i)
        self.cur[eng].append(("d", fn, tuple(deps), (j, self.dcnt[(eng, j)])))
        return (eng, ("d", j), self.dcnt[(eng, j)])

    def coll(self, fn, deps=()):
        self.ccnt += 1
        self.cur["pool"].append(("c", fn, tuple(deps), None))
        return ("pool", "c", self.ccnt)

    def barrier_toks(self):
        t = []
        for e in self.ENGS:
            if self.xcnt[e]:
                t.append((e, "x", self.xcnt[e]))
            for j in range(self.KD):
                if self.dcnt[(e, j)]:
                    t.append((e, ("d", j), self.dcnt[(e, j)]))
        if self.ccnt:
            t.append(("pool", "c", self.ccnt))
        return t

    def emit(self, nc):
        import contextlib
        with contextlib.ExitStack() as stack:
            sems = {}
            for e in self.ENGS:
                sems[(e, "x")] = stack.enter_context(nc.semaphore(f"sx_{e}"))
                for j in range(self.KD):
                    sems[(e, ("d", j))] = stack.enter_context(
                        nc.semaphore(f"sd_{e}_{j}"))
            sems[("pool", "c")] = stack.enter_context(nc.semaphore("sc_pool"))
            waited = {}

            def do_wait(eng, eo, skey, val):
                key = (eng, skey)
                if waited.get(key, 0) < val:
                    eo.wait_ge(sems[skey], val)
                    waited[key] = val

            def emit_engine(eng, ops, eo):
                for kind, fn, deps, dinfo in ops:
                    for dep in deps:
                        deng, dkind, dcount = dep
                        if dkind == "x":
                            val = dcount
                            skey = (deng, "x")
                        elif dkind == "c":
                            val = dcount
                            skey = ("pool", "c")
                        else:
                            val = 16 * dcount
                            skey = (deng, dkind)
                        do_wait(eng, eo, skey, val)
                    if kind == "d":
                        j, cnt = dinfo
                        if cnt > 1:
                            do_wait(eng, eo, (eng, ("d", j)), 16 * (cnt - 1))
                    inst = fn(eo)
                    if kind == "x":
                        inst.then_inc(sems[(eng, "x")], 1)
                    elif kind == "d":
                        j, cnt = dinfo
                        inst.then_inc(sems[(eng, ("d", j))], 16)
                    else:
                        inst.then_inc(sems[("pool", "c")], 1)

            for ph in self.phases:
                with nc.Block(ph["_name"], no_gpsimd_drain=True) as block:
                    decos = {"sp": block.sync, "act": block.scalar, "dve": block.vector,
                             "pool": block.gpsimd, "te": block.tensor}
                    for eng in self.ENGS:
                        if ph[eng]:
                            decos[eng](lambda eo, _o=ph[eng], _e=eng: emit_engine(_e, _o, eo))

def build_program():
    nc = bass.Bass("TRN2", target_bir_lowering=False)
    S = Sched()

    def din(name, shape, dt=F32):
        return nc.dram_tensor(name, shape, dt, kind="ExternalInput")

    xb = din("xb", [L, D])
    xr = din("xr", [L, D])
    winQ = din("winQ", [128, 32 * NDT * 128])
    cwc = din("cwc", [128, 3 * NDSL])
    cbv = din("cbv", [DI, 1])
    wxT = din("wxT", [DI, 96])
    wdtT = din("wdtT", [RK, DI])
    dtbv = din("dtbv", [DI, 1])
    AsbH = din("AsbH", [128, NDSL * NS])
    Dpv = din("Dpv", [DI, 1])
    woutQ = din("woutQ", [128, NDT * NDSL * 128])
    w1Q = din("w1Q", [128, NFT * NDT * 128])
    b1v = din("b1v", [FF, 1])
    w2Q = din("w2Q", [128, NDT * NFT * 128])
    b2v = din("b2v", [D, 1])
    wgT = {g: din(f"w{g}Tl", [D, D]) for g in "fic"}
    ugT = {g: din(f"u{g}Tl", [D, D]) for g in "fic"}
    bgv = {g: din(f"b{g}vl", [D, 1]) for g in "fic"}
    woT = din("woT", [D, D])
    uoT = din("uoT", [D, D])
    bov = din("bov", [D, 1])
    incv = din("incv", [D, 1])
    ohb = din("ohb", [32, 32 * 128], BF16)

    out = nc.dram_tensor("out", [L, D], F32, kind="ExternalOutput")

    sp_xps = nc.dram_tensor("sp_xps", [DI, L], F32)
    sp_zs = nc.dram_tensor("sp_zs", [DI, L], F32)
    sp_dt = nc.dram_tensor("sp_dt", [DI, L], F32)
    sp_xn = nc.dram_tensor("sp_xn", [D, L], F32)
    sp_x1 = nc.dram_tensor("sp_x1", [D, L], F32)
    sp_x2 = nc.dram_tensor("sp_x2", [D, L], F32)
    sp_gx = nc.dram_tensor("sp_gx", [3 * D, L], BF16)
    cc_in = nc.dram_tensor("cc_in", [RPC, D], F32)
    cc_out = nc.dram_tensor("cc_out", [L, D], F32)

    sbias = nc.alloc_sbuf_tensor("sbias", [128, 128], F32)
    Asb = nc.alloc_sbuf_tensor("Asb", [128, NDSL * NS], F32)
    cwS = nc.alloc_sbuf_tensor("cwS", [128, 3 * NDSL], F32)
    ident = nc.alloc_sbuf_tensor("ident", [128, 128], F32)
    smal = nc.alloc_sbuf_tensor("smal", [128, 256], F32)
    epsc = nc.alloc_sbuf_tensor("epsc", [128, 1], F32)
    nhlf = nc.alloc_sbuf_tensor("nhlf", [128, 1], F32)
    hcar = nc.alloc_sbuf_tensor("hcar", [128, NDSL * NS], F32)
    BCt = nc.alloc_sbuf_tensor("BCt", [32, L], BF16)
    dbc_dt = nc.alloc_sbuf_tensor("dbc_dt", [RK, L], BF16)
    ohS = nc.alloc_sbuf_tensor("ohS", [32, 2 * 128], BF16)
    evb = nc.alloc_sbuf_tensor("evb", [128, 4096], F32)
    wstat = nc.alloc_sbuf_tensor("wstat", [128, 4096], BF16)
    movb = nc.alloc_sbuf_tensor("movb", [128, 4096], BF16)

    AR_BYTES = 160 * KB
    arena = nc.alloc_sbuf_tensor("arena", [128, AR_BYTES], U8)
    abase = nc.lookup_mloc(arena).addr
    _n = [0]

    def at(shape, dt, off_kb):
        _n[0] += 1
        return nc.alloc_sbuf_tensor_at(f"v{_n[0]}", shape, dt,
                                       offset=abase + int(off_kb * KB))

    p0slot = [at([128, D], F32, 4 * i) for i in range(4)]
    cst = at([128, 8 * RPC], F32, 0)
    cbf = at([128, 8 * RPC], BF16, 8)
    gat = at([128, 3 * 8 * RPC], F32, 12)
    cuT = at([128, 8 * RPC], F32, 0)
    xnT = at([128, NDT * L], F32, 0)
    a2 = [at([128, 2048], F32, 0), at([128, 2048], F32, 8)]
    rld = at([128, 3 * LC], F32, 16)
    wco = at([128, LC], BF16, 32)
    u2 = [at([128, 2048], BF16, 34), at([128, 2048], BF16, 28)]
    y3a = at([128, 13 * LC], BF16, 38)
    x1c = at([128, NDT * LC], F32, 0)
    x1bf = at([128, NDT * LC], BF16, 32)
    xbt = [at([128, D], F32, 0), at([128, D], F32, 16)]
    cut = [at([128, D], F32, 4), at([128, D], F32, 20)]
    x2s = [at([128, D], F32, 8), at([128, D], F32, 24)]
    xnt5 = [at([128, D], F32, 12), at([128, D], F32, 28)]
    lnscr = [at([128, D], F32, 32), at([128, D], F32, 36)]

    lstmW = at([128, 3 * NDT * KB], BF16, 64)
    xnrT = at([128, NDT * L], BF16, 112)
    gxstg = [at([128, 2048], BF16, 144 + 4 * i) for i in range(2)]
    uos = at([128, NDT * KB], BF16, 144)
    gxld = [at([128, 3 * NDT * RPC], BF16, 112 + 12 * i) for i in range(2)]
    xn16 = at([128, NDT * L], BF16, 64)
    convb = at([128, 2056], F32, 96)
    conv2 = at([128, 2048], F32, 104.5)
    wxS = at([128, NDSL * 96], BF16, 113)
    convb2 = at([128, 2056], F32, 117)
    conv2B = at([128, 2048], F32, 125.5)
    zbuf = [at([128, 2048], F32, 134), at([128, 2048], F32, 142)]
    Bbc = at([128, NS * LC], BF16, 64)
    Cbc = at([128, NS * LC], BF16, 96)
    hblk = at([128, NS * LC], BF16, 128)
    hfb = at([128, NFT * LC], BF16, 64)
    woS = at([128, NDT * KB], BF16, 64)
    xst5 = at([128, 2 * KB], BF16, 80)

    def y3slice(d):
        if d < 13:
            return y3a[:, d * LC:(d + 1) * LC]
        return movb[:, (d - 13) * LC:(d - 12) * LC]

    PA = nc.alloc_psum_tensor("PA", [128, 2048], F32)
    PB = nc.alloc_psum_tensor("PB", [128, 2048], F32)

    bias_cols = {}

    def load_bias(name, dram, n_tiles, col0):
        toks = []
        for i in range(n_tiles):
            toks.append(S.dma("sp", sbias[:, col0 + i:col0 + i + 1],
                              dram[i * 128:(i + 1) * 128, 0:1]))
        bias_cols[name] = col0
        return toks

    def bcol(name, i):
        c = bias_cols[name] + i
        return sbias[:, c:c + 1]

    teps = None

    def ln_tile(src, dst, so, deps, scr=None):
        if scr is None:
            scr = evb[:, 3072:4096]
        mcol = smal[:, so:so + 1]
        sqcol = smal[:, so + 1:so + 2]
        vcol = smal[:, so + 2:so + 3]
        rcol = smal[:, so + 3:so + 4]
        t1 = S.op("dve", lambda e: e.reduce_sum(mcol, src, AXL.X), deps)
        t1 = S.op("dve", lambda e: e.tensor_scalar(mcol, mcol, 1.0 / D, None, ALU.mult), [t1])
        t2 = S.op("act", lambda e: e.activation(scr, src, AF.Square,
                                                accum_out=sqcol), deps)
        t3 = S.op("dve", lambda e: e.scalar_tensor_tensor(vcol, mcol, -1.0, mcol,
                                                          ALU.mult, ALU.mult), [t1])
        t4 = S.op("dve", lambda e: e.scalar_tensor_tensor(vcol, sqcol, 1.0 / D, vcol,
                                                          ALU.mult, ALU.add), [t2, t3])
        t5 = S.op("act", lambda e: e.activation(rcol, vcol, AF.Ln, bias=epsc[:, :]),
                  [t4, teps])
        t6 = S.op("act", lambda e: e.activation(rcol, rcol, AF.Exp, scale=nhlf[:, :]),
                  [t5])
        t7 = S.op("dve", lambda e: e.tensor_scalar(dst, src, mcol, rcol,
                                                   ALU.subtract, ALU.mult), [t6])
        return t7

    S.phase("P0")
    tid = S.op("pool", lambda e: e.memset(ident[:, :], 0.0))
    tid = S.op("pool", lambda e: e.affine_select(
        out=ident[:, :], in_=ident[:, :], compare_op=ALU.not_equal,
        fill=1.0, base=0, pattern=[[-1, 128]], channel_multiplier=1), [tid])
    thc = S.op("pool", lambda e: e.memset(hcar[:, :], 0.0))
    teps = S.op("pool", lambda e: e.memset(epsc[:, :], EPS))
    teps = S.op("pool", lambda e: e.memset(nhlf[:, :], -0.5), [teps])
    bias_toks = []
    bias_toks += load_bias("cb", cbv, NDSL, 0)
    bias_toks += load_bias("dtb", dtbv, NDSL, 16)
    bias_toks += load_bias("Dp", Dpv, NDSL, 32)
    bias_toks += load_bias("b1", b1v, NFT, 48)
    bias_toks += load_bias("b2", b2v, NDT, 80)
    bias_toks += load_bias("bf", bgv["f"], NDT, 88)
    bias_toks += load_bias("bi", bgv["i"], NDT, 96)
    bias_toks += load_bias("bc", bgv["c"], NDT, 104)
    bias_toks += load_bias("bo", bov, NDT, 112)
    bias_toks += load_bias("inc", incv, NDT, 120)
    tA = S.dma("sp", Asb[:, :], AsbH[:, :])
    tA = S.op("act", lambda e: e.activation(Asb[:, :], Asb[:, :], AF.Exp), [tA])
    tA = S.op("dve", lambda e: e.tensor_scalar(Asb[:, :], Asb[:, :], -1.0, None,
                                               ALU.mult), [tA])
    tcw = S.dma("sp", cwS[:, :], cwc[:, :])

    slot_tok = [[] for _ in range(4)]
    ps_tok = [[] for _ in range(NDT)]
    for tt in range(NTOK):
        s = tt % 4
        src = p0slot[s]
        tld = S.dma("sp", src[:, :], xr[tt * 128:(tt + 1) * 128, :], slot_tok[s])
        tln = ln_tile(src[:, :], src[:, :], 4 * s, [tld])
        evs = [tln]
        for j in range(NDT):
            pslc = (PA if j < 4 else PB)[:, (j % 4) * 512:(j % 4) * 512 + 128]
            ttr = S.op("te", lambda e, p=pslc, src=src, j=j: e.transpose(
                p, src[:, j * 128:(j + 1) * 128], ident[:, :]), [tln, tid] + ps_tok[j])
            tev = S.op("act", lambda e, p=pslc, j=j, tt=tt: e.copy(
                xnrT[:, j * L + tt * 128: j * L + tt * 128 + 128], p), [ttr])
            ps_tok[j] = [tev]
            evs.append(tev)
        slot_tok[s] = evs
    xnr_done = S.barrier_toks()

    S.phase("P1a")
    wg_toks = []
    for gi, g in enumerate("fic"):
        for k in range(NDT):
            wg_toks.append(S.dma("pool",
                                 lstmW[:, (gi * NDT + k) * KB:(gi * NDT + k + 1) * KB],
                                 wgT[g][k * 128:(k + 1) * 128, :]))
    gx_sp = []
    stg_tok = [[], []]
    for gi in range(3):
        for m in range(NDT):
            ps = PA if (m % 2 == 0) else PB
            mmt = []
            for k in range(NDT):
                stat = lstmW[:, (gi * NDT + k) * KB + m * 128:(gi * NDT + k) * KB + m * 128 + 128]
                for nb in range(4):
                    mmt.append(S.op("te", lambda e, ps=ps, stat=stat, k=k, nb=nb: e.matmul(
                        ps[:, nb * 512:(nb + 1) * 512], stat,
                        xnrT[:, k * L + nb * 512: k * L + (nb + 1) * 512],
                        start=(k == 0), stop=(k == NDT - 1)),
                        wg_toks + xnr_done + (gx_sp[-2:] if len(gx_sp) >= 2 else [])))
            sg = (gi * NDT + m) % 2
            tev = S.op("dve", lambda e, ps=ps, sg=sg: e.tensor_copy(
                gxstg[sg][:, :], ps[:, :]), mmt + stg_tok[sg])
            tsp = S.dma("sp", sp_gx[(gi * NDT + m) * 128:(gi * NDT + m + 1) * 128, :],
                        gxstg[sg][:, :], [tev])
            stg_tok[sg] = [tsp]
            gx_sp.append(tsp)
    gx_done = S.barrier_toks()

    S.phase("P1b")
    ug_toks = []
    for gi, g in enumerate("fic"):
        for k in range(NDT):
            ug_toks.append(S.dma("pool",
                                 lstmW[:, (gi * NDT + k) * KB:(gi * NDT + k + 1) * KB],
                                 ugT[g][k * 128:(k + 1) * 128, :], gx_done))
    uo_toks = [S.dma("pool", uos[:, k * KB:(k + 1) * KB],
                     uoT[k * 128:(k + 1) * 128, :], gx_done) for k in range(NDT)]

    cinit = []
    for m in range(NDT):
        sl = slice(m * RPC, (m + 1) * RPC)
        t0 = S.op("dve", lambda e, sl=sl: e.memset(cst[:, sl], 0.0), gx_done)
        t1 = S.op("dve", lambda e, sl=sl, m=m: e.tensor_scalar(
            cst[:, sl], cst[:, sl], bcol("inc", m), None, ALU.add), [t0] + bias_toks)
        t2 = S.op("dve", lambda e, sl=sl: e.tensor_copy(cbf[:, sl], cst[:, sl]), [t1])
        cinit.append(t2)

    cprev = cinit
    gxld_tok = [[], []]
    gslot = {}
    preslot = {}
    for t in range(8):
        sg = t % 2
        tgx = S.dma("sp", gxld[sg][:, :].rearrange("p (q c) -> p q c", q=24),
                    sp_gx[:, t * RPC:(t + 1) * RPC].rearrange("(q p) c -> p q c", p=128),
                    gx_sp + gxld_tok[sg])
        gate_toks = {}
        for gi, g in enumerate("fic"):
            gt = []
            for m in range(NDT):
                ps = PA if (m % 2 == 0) else PB
                pslc = ps[:, ((m // 2) % 2) * 512: ((m // 2) % 2) * 512 + RPC]
                skey = (m % 2, (m // 2) % 2)
                mmt = []
                for k in range(NDT):
                    stat = lstmW[:, (gi * NDT + k) * KB + m * 128:(gi * NDT + k) * KB + m * 128 + 128]
                    mmt.append(S.op("te", lambda e, p=pslc, stat=stat, k=k: e.matmul(
                        p, stat, cbf[:, k * RPC:(k + 1) * RPC],
                        start=(k == 0), stop=(k == NDT - 1)),
                        ug_toks + cprev + gslot.get(skey, [])))
                pre = evb[:, ((gi * NDT + m) % 4) * RPC:((gi * NDT + m) % 4) * RPC + RPC]
                gxsl = gxld[sg][:, (gi * NDT + m) * RPC:(gi * NDT + m + 1) * RPC]
                tadd = S.op("dve", lambda e, p=pslc, pre=pre, gxsl=gxsl: e.tensor_add(
                    pre, p, gxsl), mmt + [tgx] + preslot.get((gi * NDT + m) % 4, []))
                gslot[skey] = [tadd]
                fn = AF.Sigmoid if g in "fi" else AF.Tanh
                bn = {"f": "bf", "i": "bi", "c": "bc"}[g]
                tact = S.op("act", lambda e, pre=pre, gi=gi, m=m, fn=fn, bn=bn: e.activation(
                    gat[:, gi * 8 * RPC + m * RPC: gi * 8 * RPC + (m + 1) * RPC], pre,
                    fn, bias=bcol(bn, m)), [tadd])
                preslot[(gi * NDT + m) % 4] = [tact]
                gt.append(tact)
            gate_toks[g] = gt
        gxld_tok[sg] = [gate_toks["c"][-1]]
        cnew = []
        for m in range(NDT):
            sl = slice(m * RPC, (m + 1) * RPC)
            f_ = gat[:, 0 * 8 * RPC + m * RPC: 0 * 8 * RPC + (m + 1) * RPC]
            i_ = gat[:, 1 * 8 * RPC + m * RPC: 1 * 8 * RPC + (m + 1) * RPC]
            g_ = gat[:, 2 * 8 * RPC + m * RPC: 2 * 8 * RPC + (m + 1) * RPC]
            deps = [gate_toks["f"][m], gate_toks["i"][m], gate_toks["c"][m]]
            t1 = S.op("dve", lambda e, f_=f_, sl=sl: e.tensor_mul(
                evb[:, 1024:1280], f_, cst[:, sl]), deps)
            t2 = S.op("dve", lambda e, i_=i_, g_=g_: e.tensor_mul(
                evb[:, 1280:1536], i_, g_), deps)
            t3 = S.op("dve", lambda e, sl=sl: e.tensor_add(
                cst[:, sl], evb[:, 1024:1280], evb[:, 1280:1536]), [t1, t2])
            t4 = S.op("dve", lambda e, sl=sl: e.tensor_copy(cbf[:, sl], cst[:, sl]), [t3])
            cnew.append(t4)
        cprev = cnew

    cu_toks = []
    for m in range(NDT):
        pslc = (PA if m % 2 == 0 else PB)[:, 1024:1024 + RPC]
        mmt = []
        for k in range(NDT):
            mmt.append(S.op("te", lambda e, p=pslc, k=k, m=m: e.matmul(
                p, uos[:, k * KB + m * 128: k * KB + m * 128 + 128],
                cbf[:, k * RPC:(k + 1) * RPC],
                start=(k == 0), stop=(k == NDT - 1)), uo_toks + cprev + cu_toks[-1:]))
        tev = S.op("dve", lambda e, p=pslc, m=m: e.tensor_scalar(
            cuT[:, m * RPC:(m + 1) * RPC], p, bcol("bo", m), None, ALU.add), mmt)
        cu_toks.append(tev)

    trtoks = []
    cu_ps = [[] for _ in range(4)]
    for half in range(2):
        evtoks = []
        for j in range(NDT):
            pslc = PB[:, (j % 4) * 512:(j % 4) * 512 + 128]
            ttr = S.op("te", lambda e, p=pslc, j=j, half=half: e.transpose(
                p, cuT[:, j * RPC + half * 128: j * RPC + half * 128 + 128], ident[:, :]),
                cu_toks + trtoks + cu_ps[j % 4])
            tev = S.op("dve", lambda e, p=pslc, j=j: e.tensor_copy(
                evb[:, j * 128:(j + 1) * 128], p), [ttr])
            cu_ps[j % 4] = [tev]
            evtoks.append(tev)
        trtoks.append(S.dma("sp", cc_in[half * 128:(half + 1) * 128, :],
                            evb[:, 0:D], evtoks))
    tcoll = S.coll(lambda e: e.collective_compute(
        "AllGather", ALU.bypass, replica_groups=[list(range(NCORE))],
        ins=[cc_in.ap().opt()], outs=[cc_out.ap().opt()]), trtoks)

    S.phase("P2")
    slot_tok = [[] for _ in range(2)]
    ps_tok = [[] for _ in range(NDT)]
    for tt in range(NTOK):
        s = tt % 2
        src = evb[:, s * 1024:(s + 1) * 1024]
        tld = S.dma("sp", src, xb[tt * 128:(tt + 1) * 128, :], slot_tok[s])
        tln = ln_tile(src, src, 4 * s, [tld])
        evs = [tln]
        for j in range(NDT):
            pslc = (PA if j < 4 else PB)[:, (j % 4) * 512:(j % 4) * 512 + 128]
            ttr = S.op("te", lambda e, p=pslc, src=src, j=j: e.transpose(
                p, src[:, j * 128:(j + 1) * 128], ident[:, :]), [tln] + ps_tok[j])
            tev = S.op("dve", lambda e, p=pslc, j=j, tt=tt: e.tensor_copy(
                xnT[:, j * L + tt * 128: j * L + tt * 128 + 128], p), [ttr])
            ps_tok[j] = [tev]
            evs.append(tev)
        slot_tok[s] = evs
    ln_all = S.barrier_toks()
    xn_spill = [S.dma("sp", sp_xn[j * 128:(j + 1) * 128, :], xnT[:, j * L:(j + 1) * L],
                      ln_all) for j in range(NDT)]
    xn16_toks = [S.op("dve", lambda e, k=k: e.tensor_copy(
        xn16[:, k * L:(k + 1) * L], xnT[:, k * L:(k + 1) * L]), ln_all)
        for k in range(NDT)]
    wx_toks = [S.dma("pool", wxS[:, k * 96:(k + 1) * 96],
                     wxT[k * 128:(k + 1) * 128, :], xnr_done) for k in range(NDSL)]

    xp16_tok = [[], []]
    xproj_mm = []
    dsl_done = []
    last_evac = []
    win_tok = [[], []]
    conv_guard = [[], []]
    sg_guard = [[], []]
    zb_guard = [[], []]
    for m in range(2 * NDSL):
        wsl = wstat[:, (m % 2) * 2048:(m % 2) * 2048 + NDT * 128]
        twl = S.dma("pool", wsl, winQ[:, m * 1024:(m + 1) * 1024],
                    win_tok[m % 2] + ([tcoll] if m < 2 else []))
        mmt = []
        for k in range(NDT):
            stat = wsl[:, k * 128:(k + 1) * 128]
            for nb in range(4):
                mmt.append(S.op("te", lambda e, stat=stat, k=k, nb=nb: e.matmul(
                    PA[:, nb * 512:(nb + 1) * 512], stat,
                    xn16[:, k * L + nb * 512: k * L + (nb + 1) * 512],
                    start=(k == 0), stop=(k == NDT - 1)),
                    [twl, xn16_toks[k]] + last_evac))
        win_tok[m % 2] = [mmt[-1]]
        if m < NDSL:
            d = m
            cb_ = convb if d % 2 == 0 else convb2
            c2_ = conv2 if d % 2 == 0 else conv2B
            tz = S.op("dve", lambda e, cb_=cb_: e.memset(cb_[:, 0:2], 0.0),
                      conv_guard[d % 2])
            tev = S.op("dve", lambda e, cb_=cb_: e.tensor_copy(cb_[:, 2:2050], PA[:, :]),
                       mmt + conv_guard[d % 2])
            last_evac = [tev]
            c2 = S.op("dve", lambda e, d=d, cb_=cb_, c2_=c2_: e.tensor_scalar(
                c2_[:, :], cb_[:, 2:2050], cwS[:, 3 * d + 2:3 * d + 3], None,
                ALU.mult), [tev, tcw])
            c1 = S.op("dve", lambda e, d=d, cb_=cb_, c2_=c2_: e.scalar_tensor_tensor(
                c2_[:, :], cb_[:, 1:2049], cwS[:, 3 * d + 1:3 * d + 2], c2_[:, :],
                ALU.mult, ALU.add), [c2, tz])
            c0 = S.op("dve", lambda e, d=d, cb_=cb_, c2_=c2_: e.scalar_tensor_tensor(
                c2_[:, :], cb_[:, 0:2048], cwS[:, 3 * d + 0:3 * d + 1], c2_[:, :],
                ALU.mult, ALU.add), [c1])
            tsg = S.op("act", lambda e, d=d, c2_=c2_: e.activation(
                evb[:, 2048 + (d % 2) * 1024:2048 + (d % 2) * 1024 + 1024],
                c2_[:, 0:1024], AF.Sigmoid, bias=bcol("cb", d)),
                [c0] + bias_toks + sg_guard[d % 2])
            tsg2 = S.op("act", lambda e, d=d, c2_=c2_: e.activation(
                evb[:, 0 + (d % 2) * 1024:(d % 2) * 1024 + 1024],
                c2_[:, 1024:2048], AF.Sigmoid, bias=bcol("cb", d)),
                [c0] + bias_toks + sg_guard[d % 2])
            tsl = S.op("dve", lambda e, d=d, cb_=cb_, c2_=c2_: e.scalar_tensor_tensor(
                cb_[:, 2:1026], c2_[:, 0:1024], bcol("cb", d),
                evb[:, 2048 + (d % 2) * 1024:2048 + (d % 2) * 1024 + 1024],
                ALU.add, ALU.mult), [tsg, c0])
            tsl2 = S.op("dve", lambda e, d=d, cb_=cb_, c2_=c2_: e.scalar_tensor_tensor(
                cb_[:, 1026:2050], c2_[:, 1024:2048], bcol("cb", d),
                evb[:, (d % 2) * 1024:(d % 2) * 1024 + 1024],
                ALU.add, ALU.mult), [tsg2, tsl])
            sg_guard[d % 2] = [tsl, tsl2]
            tspl = S.dma("sp", sp_xps[d * 128:(d + 1) * 128, :], cb_[:, 2:2050],
                         [tsl, tsl2])
            t16x = S.op("dve", lambda e, d=d, cb_=cb_: e.tensor_copy(
                movb[:, (d % 2) * 2048:(d % 2) * 2048 + 2048], cb_[:, 2:2050]),
                [tsl, tsl2] + xp16_tok[d % 2])
            conv_guard[d % 2] = [tspl, t16x]
            xps = []
            for nb in range(4):
                xps.append(S.op("te", lambda e, d=d, nb=nb: e.matmul(
                    PB[0:96, nb * 512:(nb + 1) * 512],
                    wxS[:, d * 96:(d + 1) * 96],
                    movb[:, (d % 2) * 2048 + nb * 512:(d % 2) * 2048 + nb * 512 + 512],
                    start=(d == 0), stop=(d == NDSL - 1)),
                    [t16x] + wx_toks + xproj_mm[-1:]))
            xproj_mm += xps
            xp16_tok[d % 2] = [xps[-1]]
            dsl_done.append(tspl)
        else:
            d = m - NDSL
            zb = zbuf[d % 2]
            tzc = S.op("dve", lambda e, zb=zb: e.tensor_copy(zb[:, :], PA[:, 0:2048]),
                       mmt + zb_guard[d % 2])
            last_evac = [tzc]
            tsg = S.op("act", lambda e, zb=zb, d=d: e.activation(
                evb[:, 2048 + (d % 2) * 1024:2048 + (d % 2) * 1024 + 1024],
                zb[:, 0:1024], AF.Sigmoid), [tzc] + sg_guard[d % 2])
            tsg2 = S.op("act", lambda e, zb=zb, d=d: e.activation(
                evb[:, (d % 2) * 1024:(d % 2) * 1024 + 1024],
                zb[:, 1024:2048], AF.Sigmoid), [tzc] + sg_guard[d % 2])
            tsl = S.op("dve", lambda e, zb=zb, d=d: e.scalar_tensor_tensor(
                zb[:, 0:1024], zb[:, 0:1024], 0.0,
                evb[:, 2048 + (d % 2) * 1024:2048 + (d % 2) * 1024 + 1024],
                ALU.add, ALU.mult), [tsg])
            tsl2 = S.op("dve", lambda e, zb=zb, d=d: e.scalar_tensor_tensor(
                zb[:, 1024:2048], zb[:, 1024:2048], 0.0,
                evb[:, (d % 2) * 1024:(d % 2) * 1024 + 1024],
                ALU.add, ALU.mult), [tsg2])
            sg_guard[d % 2] = [tsl, tsl2]
            tspl = S.dma("sp", sp_zs[d * 128:(d + 1) * 128, :], zb[:, :], [tsl, tsl2])
            zb_guard[d % 2] = [tspl]
            dsl_done.append(tspl)
    tdbc1 = S.op("dve", lambda e: e.tensor_copy(dbc_dt[:, :], PB[0:64, :]), xproj_mm)
    tdbc2a = S.op("dve", lambda e: e.tensor_copy(movb[64:96, 0:2048], PB[64:96, :]),
                  [tdbc1] + last_evac + xp16_tok[0] + xp16_tok[1])
    tdbc2 = S.dma("sp", BCt[0:32, :], movb[64:96, 0:2048], [tdbc2a])
    twdt = S.dma("pool", wstat[0:64, 0:2048], wdtT[:, :], win_tok[0] + win_tok[1])
    dt_toks = []
    dt_ps = [[], []]
    for m in range(NDSL):
        par = m % 2
        ab_ = evb[:, 0:2048] if par == 0 else zbuf[0][:, :]
        sp_ = evb[:, 2048:4096] if par == 0 else zbuf[1][:, :]
        mmt = []
        for nb in range(4):
            mmt.append(S.op("te", lambda e, m=m, nb=nb: e.matmul(
                PA[:, nb * 512:(nb + 1) * 512],
                wstat[0:64, m * 128:(m + 1) * 128],
                dbc_dt[:, nb * 512:(nb + 1) * 512],
                start=True, stop=True),
                [twdt, tdbc1] + last_evac + sg_guard[0] + sg_guard[1] + zb_guard[0]
                + zb_guard[1] + dt_ps[0] + dt_ps[1]))
        t1 = S.op("act", lambda e, m=m, ab_=ab_: e.activation(
            ab_, PA[:, :], AF.Abs, bias=bcol("dtb", m)), mmt + dt_toks[-4:-2])
        t2 = S.op("act", lambda e, ab_=ab_: e.activation(ab_, ab_, AF.Exp,
                                                scale=-1.0), [t1])
        t3 = S.op("act", lambda e, ab_=ab_: e.activation(ab_, ab_, AF.Ln,
                                                bias=1.0), [t2])
        t4 = S.op("act", lambda e, m=m, sp_=sp_: e.activation(
            sp_, PA[:, :], AF.Relu, bias=bcol("dtb", m)), mmt + dt_toks[-4:-2])
        dt_ps[par] = [t1, t4]
        t5 = S.op("dve", lambda e, ab_=ab_, sp_=sp_: e.tensor_add(sp_, sp_, ab_),
                  [t3, t4])
        tspl = S.dma("sp", sp_dt[m * 128:(m + 1) * 128, :], sp_, [t5])
        dt_toks += [tspl, t5]
        last_evac = []
        sg_guard = [[], []]
        zb_guard = [[], []]
    p2_done = S.barrier_toks()

    S.phase("P3")
    prev_chunk = p2_done
    for ch in range(NCH):
        c0 = ch * LC
        bc_toks = []
        oh_tok = [[], []]
        for n in range(2 * NS):
            dst = (Bbc if n < NS else Cbc)[:, (n % NS) * LC:(n % NS) * LC + LC]
            osl = ohS[:, (n % 2) * 128:(n % 2) * 128 + 128]
            tohn = S.dma("sp", osl, ohb[:, n * 128:(n + 1) * 128],
                         oh_tok[n % 2] + prev_chunk)
            tmms = []
            for half in range(2):
                pslc = PB[:, 1024 + half * 512:1024 + (half + 1) * 512]
                tmms.append(S.op("te", lambda e, osl=osl, pslc=pslc, half=half: e.matmul(
                    pslc, osl, BCt[:, c0 + half * 512:c0 + (half + 1) * 512],
                    start=True, stop=True),
                    [tohn, tdbc2] + bc_toks[-1:] + prev_chunk))
            oh_tok[n % 2] = [tmms[-1]]
            tev = S.op("act", lambda e, dst=dst: e.copy(dst, PB[:, 1024:2048]), tmms)
            bc_toks.append(tev)
        bcB = bc_toks[:NS]
        bcC = bc_toks[NS:]
        y3_toks = []
        op_toks = []
        rd_guard = [[], []]
        for d in range(NDSL):
            par = d % 2
            dtb_ = rld[:, 0:LC] if par == 0 else evb[:, 2048:2048 + LC]
            xpb_ = rld[:, LC:2 * LC] if par == 0 else evb[:, 3072:3072 + LC]
            rdeps = dt_toks + dsl_done + rd_guard[par] + prev_chunk
            tdt = S.dma("sp", dtb_, sp_dt[d * 128:(d + 1) * 128, c0:c0 + LC], rdeps)
            txp = S.dma("sp", xpb_, sp_xps[d * 128:(d + 1) * 128, c0:c0 + LC], rdeps)
            tzs = S.dma("sp", rld[:, 2 * LC:3 * LC], sp_zs[d * 128:(d + 1) * 128, c0:c0 + LC],
                        rdeps + (y3_toks[-1:] if y3_toks else []))
            tw = S.op("dve", lambda e, dtb_=dtb_, xpb_=xpb_: e.tensor_mul(
                wco[:, :], dtb_, xpb_), [tdt, txp] + (y3_toks[-1:] if y3_toks else []))
            htoks = []
            for g in range(NS // 2):
                sg = g % 2
                asl = a2[sg][:, :]
                prevp = htoks[-2:] if htoks else []
                etoks = []
                for jj in range(2):
                    n = 2 * g + jj
                    etoks.append(S.op("act", lambda e, asl=asl, jj=jj, d=d, n=n, dtb_=dtb_: e.activation(
                        asl[:, jj * LC:(jj + 1) * LC], dtb_, AF.Exp,
                        scale=Asb[:, d * NS + n:d * NS + n + 1]), [tdt, tA] + prevp))
                ueng = "dve" if g % 2 == 0 else "pool"
                tu = S.op(ueng, lambda e, g=g, sg=sg: e.tensor_tensor(
                    u2[sg][:, :].rearrange("p (a b) -> p a b", a=2),
                    wco[:, None, :].broadcast_to([128, 2, LC]),
                    Bbc[:, 2 * g * LC:2 * (g + 1) * LC].rearrange("p (a b) -> p a b", a=2),
                    ALU.mult), [tw, bcB[2 * g + 1]] + (htoks[-4:-2] if len(htoks) >= 4 else []))
                for jj in range(2):
                    n = 2 * g + jj
                    hcol = hcar[:, d * NS + n:d * NS + n + 1]
                    hslc = hblk[:, n * LC:(n + 1) * LC]
                    tts = S.op("dve", lambda e, hslc=hslc, asl=asl, jj=jj, hcol=hcol, sg=sg:
                               e.tensor_tensor_scan(
                                   hslc, asl[:, jj * LC:(jj + 1) * LC],
                                   u2[sg][:, jj * LC:(jj + 1) * LC], hcol,
                                   ALU.mult, ALU.add),
                               [tu, etoks[jj], thc])
                    htoks += [tts]
            tcx = S.op("dve", lambda e, d=d: e.tensor_copy(
                hcar[:, d * NS:(d + 1) * NS],
                hblk[:, :].rearrange("p (a b) -> p a b", a=NS)[:, :, LC - 1:LC]),
                htoks)
            tp = S.op("dve", lambda e: e.tensor_mul(hblk[:, :], hblk[:, :], Cbc[:, :]),
                      htoks + [tcx, bcC[-1]])
            tr = S.op("dve", lambda e: e.tensor_add(
                hblk[:, 0:8 * LC], hblk[:, 0:8 * LC], hblk[:, 8 * LC:16 * LC]), [tp])
            tr = S.op("dve", lambda e: e.tensor_add(
                hblk[:, 0:4 * LC], hblk[:, 0:4 * LC], hblk[:, 4 * LC:8 * LC]), [tr])
            tr = S.op("dve", lambda e: e.tensor_add(
                hblk[:, 0:2 * LC], hblk[:, 0:2 * LC], hblk[:, 2 * LC:4 * LC]), [tr])
            tr = S.op("dve", lambda e: e.tensor_add(
                evb[:, 0:1024], hblk[:, 0:LC], hblk[:, LC:2 * LC]), [tr])
            ty2 = S.op("dve", lambda e, d=d, xpb_=xpb_: e.scalar_tensor_tensor(
                evb[:, 0:1024], xpb_, bcol("Dp", d), evb[:, 0:1024],
                ALU.mult, ALU.add), [tr, txp])
            ty3 = S.op("dve", lambda e, d=d: e.tensor_mul(
                y3slice(d), evb[:, 0:1024], rld[:, 2 * LC:3 * LC]), [ty2, tzs] + op_toks[-1:])
            y3_toks.append(ty3)
            rd_guard[par] = [ty3]
        for m in range(NDT):
            wsl = wstat[:, (m % 2) * 2048:(m % 2) * 2048 + 2048]
            twl = S.dma("pool", wsl, woutQ[:, m * 2048:(m + 1) * 2048],
                        op_toks[-4:-2] if len(op_toks) >= 4 else prev_chunk)
            pslc = (PA if m % 2 == 0 else PB)[:, 0:LC]
            mmt = []
            for k in range(NDSL):
                for nb in range(2):
                    mmt.append(S.op("te", lambda e, pslc=pslc, wsl=wsl, k=k, nb=nb: e.matmul(
                        pslc[:, nb * 512:(nb + 1) * 512], wsl[:, k * 128:(k + 1) * 128],
                        y3slice(k)[:, nb * 512:(nb + 1) * 512],
                        start=(k == 0), stop=(k == NDSL - 1)),
                        [twl] + y3_toks + (op_toks[-4:-3] if len(op_toks) >= 4 else [])))
            txn = S.dma("sp", evb[:, 0:LC], sp_xn[m * 128:(m + 1) * 128, c0:c0 + LC],
                        xn_spill + op_toks[-4:] + y3_toks[-1:])
            tx1 = S.op("dve", lambda e, pslc=pslc: e.tensor_add(
                evb[:, 1024:1024 + LC], pslc, evb[:, 0:LC]), mmt + [txn])
            tspl = S.dma("sp", sp_x1[m * 128:(m + 1) * 128, c0:c0 + LC],
                         evb[:, 1024:1024 + LC], [tx1])
            op_toks += [tx1, tspl]
        prev_chunk = S.barrier_toks()

    S.phase("P4")
    x1_done = prev_chunk
    x2_sp = []
    wl_tok = [[], []]
    wl2_tok = [[], []]
    for ch in range(NCH):
        c0 = ch * LC
        x1_toks = []
        for m in range(NDT):
            tld = S.dma("sp", x1c[:, m * LC:(m + 1) * LC],
                        sp_x1[m * 128:(m + 1) * 128, c0:c0 + LC], x1_done + x2_sp[-2:])
            t16 = S.op("dve", lambda e, m=m: e.tensor_copy(
                x1bf[:, m * LC:(m + 1) * LC], x1c[:, m * LC:(m + 1) * LC]), [tld])
            x1_toks += [tld, t16]
        hf_toks = []
        for m in range(NFT):
            wsl = wstat[:, (m % 2) * 2048:(m % 2) * 2048 + NDT * 128]
            twl = S.dma("pool", wsl, w1Q[:, m * 1024:(m + 1) * 1024],
                        wl_tok[m % 2] + wl2_tok[m % 2] + x2_sp[-2:])
            pslc = (PA if m % 2 == 0 else PB)[:, 0:LC]
            mmt = []
            for k in range(NDT):
                for nb in range(2):
                    mmt.append(S.op("te", lambda e, pslc=pslc, wsl=wsl, k=k, nb=nb: e.matmul(
                        pslc[:, nb * 512:(nb + 1) * 512], wsl[:, k * 128:(k + 1) * 128],
                        x1bf[:, k * LC + nb * 512:k * LC + (nb + 1) * 512],
                        start=(k == 0), stop=(k == NDT - 1)),
                        [twl] + x1_toks + hf_toks[-2:-1]))
            wl_tok[m % 2] = [mmt[-1]]
            tev = S.op("act", lambda e, pslc=pslc, m=m: e.activation(
                hfb[:, m * LC:(m + 1) * LC], pslc, AF.Gelu, bias=bcol("b1", m)), mmt)
            hf_toks.append(tev)
        w2_toks = []
        for m in range(NDT):
            wsl = wstat[:, (m % 2) * 2048:(m % 2) * 2048 + 2048]
            msl = movb[:, (m % 2) * 2048:(m % 2) * 2048 + 2048]
            wdeps = wl2_tok[m % 2] + w2_toks[-2:] + wl_tok[0] + wl_tok[1]
            twl = S.dma("pool", wsl, w2Q[:, m * 4096:m * 4096 + 2048], wdeps)
            twl2 = S.dma("pool", msl, w2Q[:, m * 4096 + 2048:(m + 1) * 4096], wdeps)
            pslc = (PA if m % 2 == 0 else PB)[:, 0:LC]
            mmt = []
            for k in range(NFT):
                stat = wsl[:, k * 128:(k + 1) * 128] if k < 16 else \
                    msl[:, (k - 16) * 128:(k - 15) * 128]
                dep = twl if k < 16 else twl2
                for nb in range(2):
                    mmt.append(S.op("te", lambda e, pslc=pslc, stat=stat, k=k, nb=nb: e.matmul(
                        pslc[:, nb * 512:(nb + 1) * 512], stat,
                        hfb[:, k * LC + nb * 512:k * LC + (nb + 1) * 512],
                        start=(k == 0), stop=(k == NFT - 1)),
                        [dep] + hf_toks + (w2_toks[-4:-3] if len(w2_toks) >= 4 else [])))
            wl2_tok[m % 2] = [mmt[-1]]
            tx2 = S.op("dve", lambda e, pslc=pslc, m=m: e.scalar_tensor_tensor(
                evb[:, 0:LC], pslc, bcol("b2", m), x1c[:, m * LC:(m + 1) * LC],
                ALU.add, ALU.add), mmt + x1_toks)
            tspl = S.dma("sp", sp_x2[m * 128:(m + 1) * 128, c0:c0 + LC], evb[:, 0:LC],
                         [tx2])
            w2_toks += [tx2, tspl]
        x2_sp += w2_toks
        x1_done = S.barrier_toks()

    S.phase("P5")
    wo_toks = [S.dma("pool", woS[:, k * KB:(k + 1) * KB],
                     woT[k * 128:(k + 1) * 128, :], x1_done) for k in range(NDT)]
    par_guard = [[], []]
    p5b_tok = [[] for _ in range(4)]
    stat_guard = [[], []]
    prev_tffn = []
    prev_tsum = []
    fin = []
    for tt in range(NTOK):
        par = tt % 2
        r0 = tt * 128
        xb_t = xbt[par]
        tldx = S.dma("sp", xb_t[:, :], xb[r0:r0 + 128, :], par_guard[par])
        tln = ln_tile(xb_t[:, :], xnt5[par][:, :], 8 + 4 * par,
                      [tldx] + par_guard[par], scr=lnscr[par][:, :])
        p5banks = [PA[:, 1024:1024 + 128], PA[:, 1536:1536 + 128],
                   PB[:, 1024:1024 + 128], PB[:, 1536:1536 + 128]]
        st_toks = []
        for j in range(NDT):
            pslc = p5banks[j % 4]
            ttr = S.op("te", lambda e, p=pslc, j=j, par=par: e.transpose(
                p, xnt5[par][:, j * 128:(j + 1) * 128], ident[:, :]),
                [tln] + p5b_tok[j % 4])
            tev = S.op("act", lambda e, p=pslc, j=j, par=par: e.copy(
                xst5[:, par * KB + j * 128:par * KB + (j + 1) * 128], p),
                [ttr] + stat_guard[par])
            p5b_tok[j % 4] = [tev]
            st_toks.append(tev)
        mmt = []
        for k in range(NDT):
            for nb in range(2):
                mmt.append(S.op("te", lambda e, k=k, nb=nb, par=par: e.matmul(
                    PA[:, nb * 512:(nb + 1) * 512],
                    xst5[:, par * KB + k * 128:par * KB + (k + 1) * 128],
                    woS[:, k * KB + nb * 512:k * KB + (nb + 1) * 512],
                    start=(k == 0), stop=(k == NDT - 1)),
                    [st_toks[k]] + wo_toks + prev_tffn))
        stat_guard[par] = [mmt[-1]]
        tcu = S.dma("sp", cut[par][:, :], cc_out[r0:r0 + 128, :],
                    [tcoll] + par_guard[par])
        tffn = S.op("dve", lambda e, par=par: e.tensor_add(
            evb[:, par * 2048:par * 2048 + D], PA[:, 0:D], cut[par][:, :]),
            mmt + [tcu] + par_guard[par])
        prev_tffn = [tffn]
        ttanh = S.op("act", lambda e, par=par: e.activation(
            evb[:, par * 2048:par * 2048 + D], evb[:, par * 2048:par * 2048 + D],
            AF.Tanh), [tffn])
        tx2l = S.dma("sp", x2s[par][:, :].rearrange("p (j c) -> p j c", j=NDT),
                     sp_x2[:, r0:r0 + 128].rearrange("(j p) c -> p j c", p=128),
                     x2_sp + par_guard[par])
        tr2 = []
        for j in range(NDT):
            pslc = PB[:, j * 128:(j + 1) * 128]
            tr2.append(S.op("te", lambda e, p=pslc, j=j, par=par: e.transpose(
                p, x2s[par][:, j * 128:(j + 1) * 128], ident[:, :]),
                [tx2l] + prev_tsum))
        tsum = S.op("dve", lambda e, par=par: e.tensor_add(
            evb[:, par * 2048 + 1024:par * 2048 + 1024 + D], PB[:, 0:D],
            xbt[par][:, :]), tr2 + [tldx, tln] + par_guard[par])
        prev_tsum = [tsum]
        tout = S.op("dve", lambda e, par=par: e.tensor_add(
            evb[:, par * 2048 + 1024:par * 2048 + 1024 + D],
            evb[:, par * 2048 + 1024:par * 2048 + 1024 + D],
            evb[:, par * 2048:par * 2048 + D]), [tsum, ttanh])
        tst = S.dma("sp", out[r0:r0 + 128, :], evb[:, par * 2048 + 1024:par * 2048 + 1024 + D],
                    [tout])
        par_guard[par] = [tst, tout, mmt[-1], tr2[-1], tsum]
        fin = [tst]
    S.op("sp", lambda e: e.nop(), fin)

    S.emit(nc)
    return nc


def kernel(**inputs):
    f = np.float32
    x = np.ascontiguousarray(np.asarray(inputs["L0_attn_x"], f))

    def T(a):
        return np.ascontiguousarray(np.asarray(a, f).T)

    def col(a):
        return np.ascontiguousarray(np.asarray(a, f).reshape(-1, 1))

    def qform(wT, K, M):
        return np.ascontiguousarray(
            wT.reshape(K, 128, M, 128).transpose(1, 2, 0, 3).reshape(128, M * K * 128))

    import ml_dtypes
    onehot = np.zeros((32, 32 * 128), np.float32)
    for n in range(32):
        onehot[n, n * 128:(n + 1) * 128] = 1.0
    onehot = onehot.astype(ml_dtypes.bfloat16)

    common = {
        "winQ": qform(T(inputs["L0_attn_in_proj_w"]), NDT, 32),
        "cwc": np.ascontiguousarray(
            np.asarray(inputs["L0_attn_conv_w"], f).reshape(16, 128, 3)
            .transpose(1, 0, 2).reshape(128, 48)),
        "cbv": col(inputs["L0_attn_conv_b"]),
        "wxT": T(inputs["L0_attn_x_proj_w"]),
        "wdtT": T(inputs["L0_attn_dt_proj_w"]),
        "dtbv": col(inputs["L0_attn_dt_proj_b"]),
        "AsbH": np.ascontiguousarray(
            np.asarray(inputs["L0_attn_A_log"], f).reshape(16, 128, 16)
            .transpose(1, 0, 2).reshape(128, 256)),
        "Dpv": col(inputs["L0_attn_Dp"]),
        "woutQ": qform(T(inputs["L0_attn_out_proj_w"]), NDSL, NDT),
        "w1Q": qform(T(inputs["L0_attn_w1"]), NDT, NFT),
        "b1v": col(inputs["L0_attn_b1"]),
        "w2Q": qform(T(inputs["L0_attn_w2"]), NFT, NDT),
        "b2v": col(inputs["L0_attn_b2"]),
        "wfTl": T(inputs["L0_ffn_wf"]), "ufTl": T(inputs["L0_ffn_uf"]),
        "bfvl": col(inputs["L0_ffn_bf"]),
        "wiTl": T(inputs["L0_ffn_wi"]), "uiTl": T(inputs["L0_ffn_ui"]),
        "bivl": col(inputs["L0_ffn_bi"]),
        "wcTl": T(inputs["L0_ffn_wc"]), "ucTl": T(inputs["L0_ffn_uc"]),
        "bcvl": col(inputs["L0_ffn_bc"]),
        "woT": T(inputs["L0_ffn_wo"]), "uoT": T(inputs["L0_ffn_uo"]),
        "bov": col(inputs["L0_ffn_bo"]),
        "incv": col(inputs["L0_ffn_in_c"]),
        "ohb": onehot,
    }
    in_maps = []
    for b in range(NCORE):
        m = dict(common)
        m["xb"] = np.ascontiguousarray(x[b])
        m["xr"] = np.ascontiguousarray(x[:, b * RPC:(b + 1) * RPC, :].reshape(L, D))
        in_maps.append(m)

    nc = build_program()
    res = run_bass_kernel_spmd(nc, in_maps, core_ids=list(range(NCORE)))
    return np.stack([np.asarray(res.results[b]["out"], np.float32)
                     for b in range(NCORE)], axis=0)


# version 15
# speedup vs baseline: 1.0588x; 1.0588x over previous
import numpy as np
import concourse.bass as bass
import concourse.mybir as mybir
from concourse.bass_utils import run_bass_kernel_spmd

F32 = mybir.dt.float32
BF16 = mybir.dt.bfloat16
U8 = mybir.dt.uint8
AF = mybir.ActivationFunctionType
ALU = mybir.AluOpType
AXL = mybir.AxisListType

NCORE = 8
L = 2048
D = 1024
DI = 2048
NS = 16
RK = 64
FF = 4096
RPC = 256
NDSL = 16
NDT = 8
NFT = 32
NTOK = 16
LC = 1024
NCH = 2
EPS = 1e-5
KB = 1024


class Sched:
    ENGS = ("sp", "act", "dve", "pool", "te")
    KD = 12

    def __init__(self):
        self.phases = []
        self.cur = None
        self.xcnt = {e: 0 for e in self.ENGS}
        self.dcnt = {(e, j): 0 for e in self.ENGS for j in range(self.KD)}
        self.dseq = {e: 0 for e in self.ENGS}
        self.ccnt = 0

    def phase(self, name):
        self.cur = {e: [] for e in self.ENGS}
        self.cur["_name"] = name
        self.phases.append(self.cur)

    def op(self, eng, fn, deps=()):
        self.xcnt[eng] += 1
        self.cur[eng].append(("x", fn, tuple(deps), None))
        return (eng, "x", self.xcnt[eng])

    def dma(self, eng, out, in_, deps=()):
        j = self.dseq[eng] % self.KD
        self.dseq[eng] += 1
        self.dcnt[(eng, j)] += 1
        fn = lambda e, o=out, i=in_: e.dma_start(out=o, in_=i)
        self.cur[eng].append(("d", fn, tuple(deps), (j, self.dcnt[(eng, j)])))
        return (eng, ("d", j), self.dcnt[(eng, j)])

    def coll(self, fn, deps=()):
        self.ccnt += 1
        self.cur["pool"].append(("c", fn, tuple(deps), None))
        return ("pool", "c", self.ccnt)

    def barrier_toks(self):
        t = []
        for e in self.ENGS:
            if self.xcnt[e]:
                t.append((e, "x", self.xcnt[e]))
            for j in range(self.KD):
                if self.dcnt[(e, j)]:
                    t.append((e, ("d", j), self.dcnt[(e, j)]))
        if self.ccnt:
            t.append(("pool", "c", self.ccnt))
        return t

    def emit(self, nc):
        import contextlib
        with contextlib.ExitStack() as stack:
            sems = {}
            for e in self.ENGS:
                sems[(e, "x")] = stack.enter_context(nc.semaphore(f"sx_{e}"))
                for j in range(self.KD):
                    sems[(e, ("d", j))] = stack.enter_context(
                        nc.semaphore(f"sd_{e}_{j}"))
            sems[("pool", "c")] = stack.enter_context(nc.semaphore("sc_pool"))
            waited = {}

            def do_wait(eng, eo, skey, val):
                key = (eng, skey)
                if waited.get(key, 0) < val:
                    eo.wait_ge(sems[skey], val)
                    waited[key] = val

            def emit_engine(eng, ops, eo):
                for kind, fn, deps, dinfo in ops:
                    for dep in deps:
                        deng, dkind, dcount = dep
                        if dkind == "x":
                            val = dcount
                            skey = (deng, "x")
                        elif dkind == "c":
                            val = dcount
                            skey = ("pool", "c")
                        else:
                            val = 16 * dcount
                            skey = (deng, dkind)
                        do_wait(eng, eo, skey, val)
                    if kind == "d":
                        j, cnt = dinfo
                        if cnt > 1:
                            do_wait(eng, eo, (eng, ("d", j)), 16 * (cnt - 1))
                    inst = fn(eo)
                    if kind == "x":
                        inst.then_inc(sems[(eng, "x")], 1)
                    elif kind == "d":
                        j, cnt = dinfo
                        inst.then_inc(sems[(eng, ("d", j))], 16)
                    else:
                        inst.then_inc(sems[("pool", "c")], 1)

            for ph in self.phases:
                with nc.Block(ph["_name"], no_gpsimd_drain=True) as block:
                    decos = {"sp": block.sync, "act": block.scalar, "dve": block.vector,
                             "pool": block.gpsimd, "te": block.tensor}
                    for eng in self.ENGS:
                        if ph[eng]:
                            decos[eng](lambda eo, _o=ph[eng], _e=eng: emit_engine(_e, _o, eo))

def build_program():
    nc = bass.Bass("TRN2", target_bir_lowering=False)
    S = Sched()

    def din(name, shape, dt=F32):
        return nc.dram_tensor(name, shape, dt, kind="ExternalInput")

    xb = din("xb", [L, D])
    xr = din("xr", [L, D])
    winQ = din("winQ", [128, 32 * NDT * 128])
    cwc = din("cwc", [128, 3 * NDSL])
    cbv = din("cbv", [DI, 1])
    wxT = din("wxT", [DI, 96])
    wdtT = din("wdtT", [RK, DI])
    dtbv = din("dtbv", [DI, 1])
    AsbH = din("AsbH", [128, NDSL * NS])
    Dpv = din("Dpv", [DI, 1])
    woutQ = din("woutQ", [128, NDT * NDSL * 128])
    w1Q = din("w1Q", [128, NFT * NDT * 128])
    b1v = din("b1v", [FF, 1])
    w2Q = din("w2Q", [128, NDT * NFT * 128])
    b2v = din("b2v", [D, 1])
    wgT = {g: din(f"w{g}Tl", [D, D]) for g in "fic"}
    ugT = {g: din(f"u{g}Tl", [D, D]) for g in "fic"}
    bgv = {g: din(f"b{g}vl", [D, 1]) for g in "fic"}
    woT = din("woT", [D, D])
    uoT = din("uoT", [D, D])
    bov = din("bov", [D, 1])
    incv = din("incv", [D, 1])
    ohb = din("ohb", [32, 32 * 128], BF16)

    out = nc.dram_tensor("out", [L, D], F32, kind="ExternalOutput")

    sp_xps = nc.dram_tensor("sp_xps", [DI, L], F32)
    sp_zs = nc.dram_tensor("sp_zs", [DI, L], F32)
    sp_dt = nc.dram_tensor("sp_dt", [DI, L], F32)
    sp_xn = nc.dram_tensor("sp_xn", [D, L], F32)
    sp_x1 = nc.dram_tensor("sp_x1", [D, L], F32)
    sp_x2 = nc.dram_tensor("sp_x2", [D, L], F32)
    sp_gx = nc.dram_tensor("sp_gx", [3 * D, L], BF16)
    cc_in = nc.dram_tensor("cc_in", [RPC, D], F32)
    cc_out = nc.dram_tensor("cc_out", [L, D], F32)

    sbias = nc.alloc_sbuf_tensor("sbias", [128, 128], F32)
    Asb = nc.alloc_sbuf_tensor("Asb", [128, NDSL * NS], F32)
    cwS = nc.alloc_sbuf_tensor("cwS", [128, 3 * NDSL], F32)
    ident = nc.alloc_sbuf_tensor("ident", [128, 128], F32)
    smal = nc.alloc_sbuf_tensor("smal", [128, 256], F32)
    epsc = nc.alloc_sbuf_tensor("epsc", [128, 1], F32)
    nhlf = nc.alloc_sbuf_tensor("nhlf", [128, 1], F32)
    hcar = nc.alloc_sbuf_tensor("hcar", [128, NDSL * NS], F32)
    BCt = nc.alloc_sbuf_tensor("BCt", [32, L], BF16)
    dbc_dt = nc.alloc_sbuf_tensor("dbc_dt", [RK, L], BF16)
    ohS = nc.alloc_sbuf_tensor("ohS", [32, 2 * 128], BF16)
    evb = nc.alloc_sbuf_tensor("evb", [128, 4096], F32)
    wstat = nc.alloc_sbuf_tensor("wstat", [128, 4096], BF16)
    movb = nc.alloc_sbuf_tensor("movb", [128, 4096], BF16)

    AR_BYTES = 160 * KB
    arena = nc.alloc_sbuf_tensor("arena", [128, AR_BYTES], U8)
    abase = nc.lookup_mloc(arena).addr
    _n = [0]

    def at(shape, dt, off_kb):
        _n[0] += 1
        return nc.alloc_sbuf_tensor_at(f"v{_n[0]}", shape, dt,
                                       offset=abase + int(off_kb * KB))

    p0slot = [at([128, D], F32, 4 * i) for i in range(4)]
    cst = at([128, 8 * RPC], F32, 0)
    cbf = at([128, 8 * RPC], BF16, 8)
    gat = at([128, 3 * 8 * RPC], F32, 12)
    cuT = at([128, 8 * RPC], F32, 0)
    xnT = at([128, NDT * L], F32, 0)
    a2 = [at([128, 2048], F32, 0), at([128, 2048], F32, 8)]
    rld = at([128, 3 * LC], F32, 16)
    ycol = at([128, LC], F32, 28)
    wco = at([128, LC], BF16, 32)
    u2 = at([128, 2048], BF16, 34)
    y3a = at([128, 13 * LC], BF16, 38)
    x1c = at([128, NDT * LC], F32, 0)
    x1bf = at([128, NDT * LC], BF16, 32)
    xbt = [at([128, D], F32, 0), at([128, D], F32, 16)]
    cut = [at([128, D], F32, 4), at([128, D], F32, 20)]
    x2s = [at([128, D], F32, 8), at([128, D], F32, 24)]
    xnt5 = [at([128, D], F32, 12), at([128, D], F32, 28)]
    lnscr = [at([128, D], F32, 32), at([128, D], F32, 36)]

    lstmW = at([128, 3 * NDT * KB], BF16, 64)
    xnrT = at([128, NDT * L], BF16, 112)
    gxstg = [at([128, 2048], BF16, 144 + 4 * i) for i in range(2)]
    uos = at([128, NDT * KB], BF16, 144)
    gxld = [at([128, 3 * NDT * RPC], BF16, 112 + 12 * i) for i in range(2)]
    xn16 = at([128, NDT * L], BF16, 64)
    convb = at([128, 2056], F32, 96)
    conv2 = at([128, 2048], F32, 104.5)
    wxS = at([128, NDSL * 96], BF16, 113)
    convb2 = at([128, 2056], F32, 117)
    conv2B = at([128, 2048], F32, 125.5)
    zbuf = [at([128, 2048], F32, 134), at([128, 2048], F32, 142)]
    Bbc = at([128, NS * LC], BF16, 64)
    Cbc = at([128, NS * LC], BF16, 96)
    hblk = at([128, NS * LC], BF16, 128)
    hfb = at([128, NFT * LC], BF16, 64)
    woS = at([128, NDT * KB], BF16, 64)
    xst5 = at([128, 2 * KB], BF16, 80)

    def y3slice(d):
        if d < 13:
            return y3a[:, d * LC:(d + 1) * LC]
        return movb[:, (d - 13) * LC:(d - 12) * LC]

    PA = nc.alloc_psum_tensor("PA", [128, 2048], F32)
    PB = nc.alloc_psum_tensor("PB", [128, 2048], F32)

    bias_cols = {}

    def load_bias(name, dram, n_tiles, col0):
        toks = []
        for i in range(n_tiles):
            toks.append(S.dma("sp", sbias[:, col0 + i:col0 + i + 1],
                              dram[i * 128:(i + 1) * 128, 0:1]))
        bias_cols[name] = col0
        return toks

    def bcol(name, i):
        c = bias_cols[name] + i
        return sbias[:, c:c + 1]

    teps = None

    def ln_tile(src, dst, so, deps, scr=None):
        if scr is None:
            scr = evb[:, 3072:4096]
        mcol = smal[:, so:so + 1]
        sqcol = smal[:, so + 1:so + 2]
        vcol = smal[:, so + 2:so + 3]
        rcol = smal[:, so + 3:so + 4]
        t1 = S.op("dve", lambda e: e.reduce_sum(mcol, src, AXL.X), deps)
        t1 = S.op("dve", lambda e: e.tensor_scalar(mcol, mcol, 1.0 / D, None, ALU.mult), [t1])
        t2 = S.op("act", lambda e: e.activation(scr, src, AF.Square,
                                                accum_out=sqcol), deps)
        t3 = S.op("dve", lambda e: e.scalar_tensor_tensor(vcol, mcol, -1.0, mcol,
                                                          ALU.mult, ALU.mult), [t1])
        t4 = S.op("dve", lambda e: e.scalar_tensor_tensor(vcol, sqcol, 1.0 / D, vcol,
                                                          ALU.mult, ALU.add), [t2, t3])
        t5 = S.op("act", lambda e: e.activation(rcol, vcol, AF.Ln, bias=epsc[:, :]),
                  [t4, teps])
        t6 = S.op("act", lambda e: e.activation(rcol, rcol, AF.Exp, scale=nhlf[:, :]),
                  [t5])
        t7 = S.op("dve", lambda e: e.tensor_scalar(dst, src, mcol, rcol,
                                                   ALU.subtract, ALU.mult), [t6])
        return t7

    S.phase("P0")
    tid = S.op("pool", lambda e: e.memset(ident[:, :], 0.0))
    tid = S.op("pool", lambda e: e.affine_select(
        out=ident[:, :], in_=ident[:, :], compare_op=ALU.not_equal,
        fill=1.0, base=0, pattern=[[-1, 128]], channel_multiplier=1), [tid])
    thc = S.op("pool", lambda e: e.memset(hcar[:, :], 0.0))
    teps = S.op("pool", lambda e: e.memset(epsc[:, :], EPS))
    teps = S.op("pool", lambda e: e.memset(nhlf[:, :], -0.5), [teps])
    bias_toks = []
    bias_toks += load_bias("cb", cbv, NDSL, 0)
    bias_toks += load_bias("dtb", dtbv, NDSL, 16)
    bias_toks += load_bias("Dp", Dpv, NDSL, 32)
    bias_toks += load_bias("b1", b1v, NFT, 48)
    bias_toks += load_bias("b2", b2v, NDT, 80)
    bias_toks += load_bias("bf", bgv["f"], NDT, 88)
    bias_toks += load_bias("bi", bgv["i"], NDT, 96)
    bias_toks += load_bias("bc", bgv["c"], NDT, 104)
    bias_toks += load_bias("bo", bov, NDT, 112)
    bias_toks += load_bias("inc", incv, NDT, 120)
    tA = S.dma("sp", Asb[:, :], AsbH[:, :])
    tA = S.op("act", lambda e: e.activation(Asb[:, :], Asb[:, :], AF.Exp), [tA])
    tA = S.op("dve", lambda e: e.tensor_scalar(Asb[:, :], Asb[:, :], -1.0, None,
                                               ALU.mult), [tA])
    tcw = S.dma("sp", cwS[:, :], cwc[:, :])

    slot_tok = [[] for _ in range(4)]
    ps_tok = [[] for _ in range(NDT)]
    for tt in range(NTOK):
        s = tt % 4
        src = p0slot[s]
        tld = S.dma("sp", src[:, :], xr[tt * 128:(tt + 1) * 128, :], slot_tok[s])
        tln = ln_tile(src[:, :], src[:, :], 4 * s, [tld])
        evs = [tln]
        for j in range(NDT):
            pslc = (PA if j < 4 else PB)[:, (j % 4) * 512:(j % 4) * 512 + 128]
            ttr = S.op("te", lambda e, p=pslc, src=src, j=j: e.transpose(
                p, src[:, j * 128:(j + 1) * 128], ident[:, :]), [tln, tid] + ps_tok[j])
            tev = S.op("act", lambda e, p=pslc, j=j, tt=tt: e.copy(
                xnrT[:, j * L + tt * 128: j * L + tt * 128 + 128], p), [ttr])
            ps_tok[j] = [tev]
            evs.append(tev)
        slot_tok[s] = evs
    xnr_done = S.barrier_toks()

    S.phase("P1a")
    wg_toks = []
    for gi, g in enumerate("fic"):
        for k in range(NDT):
            wg_toks.append(S.dma("pool",
                                 lstmW[:, (gi * NDT + k) * KB:(gi * NDT + k + 1) * KB],
                                 wgT[g][k * 128:(k + 1) * 128, :]))
    gx_sp = []
    stg_tok = [[], []]
    for gi in range(3):
        for m in range(NDT):
            ps = PA if (m % 2 == 0) else PB
            mmt = []
            for k in range(NDT):
                stat = lstmW[:, (gi * NDT + k) * KB + m * 128:(gi * NDT + k) * KB + m * 128 + 128]
                for nb in range(4):
                    mmt.append(S.op("te", lambda e, ps=ps, stat=stat, k=k, nb=nb: e.matmul(
                        ps[:, nb * 512:(nb + 1) * 512], stat,
                        xnrT[:, k * L + nb * 512: k * L + (nb + 1) * 512],
                        start=(k == 0), stop=(k == NDT - 1)),
                        wg_toks + xnr_done + (gx_sp[-2:] if len(gx_sp) >= 2 else [])))
            sg = (gi * NDT + m) % 2
            tev = S.op("dve", lambda e, ps=ps, sg=sg: e.tensor_copy(
                gxstg[sg][:, :], ps[:, :]), mmt + stg_tok[sg])
            tsp = S.dma("sp", sp_gx[(gi * NDT + m) * 128:(gi * NDT + m + 1) * 128, :],
                        gxstg[sg][:, :], [tev])
            stg_tok[sg] = [tsp]
            gx_sp.append(tsp)
    gx_done = S.barrier_toks()

    S.phase("P1b")
    ug_toks = []
    for gi, g in enumerate("fic"):
        for k in range(NDT):
            ug_toks.append(S.dma("pool",
                                 lstmW[:, (gi * NDT + k) * KB:(gi * NDT + k + 1) * KB],
                                 ugT[g][k * 128:(k + 1) * 128, :], gx_done))
    uo_toks = [S.dma("pool", uos[:, k * KB:(k + 1) * KB],
                     uoT[k * 128:(k + 1) * 128, :], gx_done) for k in range(NDT)]

    cinit = []
    for m in range(NDT):
        sl = slice(m * RPC, (m + 1) * RPC)
        t0 = S.op("dve", lambda e, sl=sl: e.memset(cst[:, sl], 0.0), gx_done)
        t1 = S.op("dve", lambda e, sl=sl, m=m: e.tensor_scalar(
            cst[:, sl], cst[:, sl], bcol("inc", m), None, ALU.add), [t0] + bias_toks)
        t2 = S.op("dve", lambda e, sl=sl: e.tensor_copy(cbf[:, sl], cst[:, sl]), [t1])
        cinit.append(t2)

    cprev = cinit
    gxld_tok = [[], []]
    gslot = {}
    preslot = {}
    for t in range(8):
        sg = t % 2
        tgx = S.dma("sp", gxld[sg][:, :].rearrange("p (q c) -> p q c", q=24),
                    sp_gx[:, t * RPC:(t + 1) * RPC].rearrange("(q p) c -> p q c", p=128),
                    gx_sp + gxld_tok[sg])
        gate_toks = {}
        for gi, g in enumerate("fic"):
            gt = []
            for m in range(NDT):
                ps = PA if (m % 2 == 0) else PB
                pslc = ps[:, ((m // 2) % 2) * 512: ((m // 2) % 2) * 512 + RPC]
                skey = (m % 2, (m // 2) % 2)
                mmt = []
                for k in range(NDT):
                    stat = lstmW[:, (gi * NDT + k) * KB + m * 128:(gi * NDT + k) * KB + m * 128 + 128]
                    mmt.append(S.op("te", lambda e, p=pslc, stat=stat, k=k: e.matmul(
                        p, stat, cbf[:, k * RPC:(k + 1) * RPC],
                        start=(k == 0), stop=(k == NDT - 1)),
                        ug_toks + cprev + gslot.get(skey, [])))
                pre = evb[:, ((gi * NDT + m) % 4) * RPC:((gi * NDT + m) % 4) * RPC + RPC]
                gxsl = gxld[sg][:, (gi * NDT + m) * RPC:(gi * NDT + m + 1) * RPC]
                tadd = S.op("dve", lambda e, p=pslc, pre=pre, gxsl=gxsl: e.tensor_add(
                    pre, p, gxsl), mmt + [tgx] + preslot.get((gi * NDT + m) % 4, []))
                gslot[skey] = [tadd]
                fn = AF.Sigmoid if g in "fi" else AF.Tanh
                bn = {"f": "bf", "i": "bi", "c": "bc"}[g]
                tact = S.op("act", lambda e, pre=pre, gi=gi, m=m, fn=fn, bn=bn: e.activation(
                    gat[:, gi * 8 * RPC + m * RPC: gi * 8 * RPC + (m + 1) * RPC], pre,
                    fn, bias=bcol(bn, m)), [tadd])
                preslot[(gi * NDT + m) % 4] = [tact]
                gt.append(tact)
            gate_toks[g] = gt
        gxld_tok[sg] = [gate_toks["c"][-1]]
        cnew = []
        for m in range(NDT):
            sl = slice(m * RPC, (m + 1) * RPC)
            f_ = gat[:, 0 * 8 * RPC + m * RPC: 0 * 8 * RPC + (m + 1) * RPC]
            i_ = gat[:, 1 * 8 * RPC + m * RPC: 1 * 8 * RPC + (m + 1) * RPC]
            g_ = gat[:, 2 * 8 * RPC + m * RPC: 2 * 8 * RPC + (m + 1) * RPC]
            deps = [gate_toks["f"][m], gate_toks["i"][m], gate_toks["c"][m]]
            t1 = S.op("dve", lambda e, f_=f_, sl=sl: e.tensor_mul(
                evb[:, 1024:1280], f_, cst[:, sl]), deps)
            t2 = S.op("dve", lambda e, i_=i_, g_=g_: e.tensor_mul(
                evb[:, 1280:1536], i_, g_), deps)
            t3 = S.op("dve", lambda e, sl=sl: e.tensor_add(
                cst[:, sl], evb[:, 1024:1280], evb[:, 1280:1536]), [t1, t2])
            t4 = S.op("dve", lambda e, sl=sl: e.tensor_copy(cbf[:, sl], cst[:, sl]), [t3])
            cnew.append(t4)
        cprev = cnew

    cu_toks = []
    for m in range(NDT):
        pslc = (PA if m % 2 == 0 else PB)[:, 1024:1024 + RPC]
        mmt = []
        for k in range(NDT):
            mmt.append(S.op("te", lambda e, p=pslc, k=k, m=m: e.matmul(
                p, uos[:, k * KB + m * 128: k * KB + m * 128 + 128],
                cbf[:, k * RPC:(k + 1) * RPC],
                start=(k == 0), stop=(k == NDT - 1)), uo_toks + cprev + cu_toks[-1:]))
        tev = S.op("dve", lambda e, p=pslc, m=m: e.tensor_scalar(
            cuT[:, m * RPC:(m + 1) * RPC], p, bcol("bo", m), None, ALU.add), mmt)
        cu_toks.append(tev)

    trtoks = []
    cu_ps = [[] for _ in range(4)]
    for half in range(2):
        evtoks = []
        for j in range(NDT):
            pslc = PB[:, (j % 4) * 512:(j % 4) * 512 + 128]
            ttr = S.op("te", lambda e, p=pslc, j=j, half=half: e.transpose(
                p, cuT[:, j * RPC + half * 128: j * RPC + half * 128 + 128], ident[:, :]),
                cu_toks + trtoks + cu_ps[j % 4])
            tev = S.op("dve", lambda e, p=pslc, j=j: e.tensor_copy(
                evb[:, j * 128:(j + 1) * 128], p), [ttr])
            cu_ps[j % 4] = [tev]
            evtoks.append(tev)
        trtoks.append(S.dma("sp", cc_in[half * 128:(half + 1) * 128, :],
                            evb[:, 0:D], evtoks))
    tcoll = S.coll(lambda e: e.collective_compute(
        "AllGather", ALU.bypass, replica_groups=[list(range(NCORE))],
        ins=[cc_in.ap().opt()], outs=[cc_out.ap().opt()]), trtoks)

    S.phase("P2")
    slot_tok = [[] for _ in range(2)]
    ps_tok = [[] for _ in range(NDT)]
    for tt in range(NTOK):
        s = tt % 2
        src = evb[:, s * 1024:(s + 1) * 1024]
        tld = S.dma("sp", src, xb[tt * 128:(tt + 1) * 128, :], slot_tok[s])
        tln = ln_tile(src, src, 4 * s, [tld])
        evs = [tln]
        for j in range(NDT):
            pslc = (PA if j < 4 else PB)[:, (j % 4) * 512:(j % 4) * 512 + 128]
            ttr = S.op("te", lambda e, p=pslc, src=src, j=j: e.transpose(
                p, src[:, j * 128:(j + 1) * 128], ident[:, :]), [tln] + ps_tok[j])
            tev = S.op("dve", lambda e, p=pslc, j=j, tt=tt: e.tensor_copy(
                xnT[:, j * L + tt * 128: j * L + tt * 128 + 128], p), [ttr])
            ps_tok[j] = [tev]
            evs.append(tev)
        slot_tok[s] = evs
    ln_all = S.barrier_toks()
    xn_spill = [S.dma("sp", sp_xn[j * 128:(j + 1) * 128, :], xnT[:, j * L:(j + 1) * L],
                      ln_all) for j in range(NDT)]
    xn16_toks = [S.op("dve", lambda e, k=k: e.tensor_copy(
        xn16[:, k * L:(k + 1) * L], xnT[:, k * L:(k + 1) * L]), ln_all)
        for k in range(NDT)]
    wx_toks = [S.dma("pool", wxS[:, k * 96:(k + 1) * 96],
                     wxT[k * 128:(k + 1) * 128, :], xnr_done) for k in range(NDSL)]

    xp16_tok = [[], []]
    xproj_mm = []
    dsl_done = []
    last_evac = []
    win_tok = [[], []]
    conv_guard = [[], []]
    sg_guard = [[], []]
    zb_guard = [[], []]
    for m in range(2 * NDSL):
        wsl = wstat[:, (m % 2) * 2048:(m % 2) * 2048 + NDT * 128]
        twl = S.dma("pool", wsl, winQ[:, m * 1024:(m + 1) * 1024],
                    win_tok[m % 2] + ([tcoll] if m < 2 else []))
        mmt = []
        for k in range(NDT):
            stat = wsl[:, k * 128:(k + 1) * 128]
            for nb in range(4):
                mmt.append(S.op("te", lambda e, stat=stat, k=k, nb=nb: e.matmul(
                    PA[:, nb * 512:(nb + 1) * 512], stat,
                    xn16[:, k * L + nb * 512: k * L + (nb + 1) * 512],
                    start=(k == 0), stop=(k == NDT - 1)),
                    [twl, xn16_toks[k]] + last_evac))
        win_tok[m % 2] = [mmt[-1]]
        if m < NDSL:
            d = m
            cb_ = convb if d % 2 == 0 else convb2
            c2_ = conv2 if d % 2 == 0 else conv2B
            tz = S.op("dve", lambda e, cb_=cb_: e.memset(cb_[:, 0:2], 0.0),
                      conv_guard[d % 2])
            tev = S.op("dve", lambda e, cb_=cb_: e.tensor_copy(cb_[:, 2:2050], PA[:, :]),
                       mmt + conv_guard[d % 2])
            last_evac = [tev]
            c2 = S.op("dve", lambda e, d=d, cb_=cb_, c2_=c2_: e.tensor_scalar(
                c2_[:, :], cb_[:, 2:2050], cwS[:, 3 * d + 2:3 * d + 3], None,
                ALU.mult), [tev, tcw])
            c1 = S.op("dve", lambda e, d=d, cb_=cb_, c2_=c2_: e.scalar_tensor_tensor(
                c2_[:, :], cb_[:, 1:2049], cwS[:, 3 * d + 1:3 * d + 2], c2_[:, :],
                ALU.mult, ALU.add), [c2, tz])
            c0 = S.op("dve", lambda e, d=d, cb_=cb_, c2_=c2_: e.scalar_tensor_tensor(
                c2_[:, :], cb_[:, 0:2048], cwS[:, 3 * d + 0:3 * d + 1], c2_[:, :],
                ALU.mult, ALU.add), [c1])
            tsg = S.op("act", lambda e, d=d, c2_=c2_: e.activation(
                evb[:, 2048 + (d % 2) * 1024:2048 + (d % 2) * 1024 + 1024],
                c2_[:, 0:1024], AF.Sigmoid, bias=bcol("cb", d)),
                [c0] + bias_toks + sg_guard[d % 2])
            tsg2 = S.op("act", lambda e, d=d, c2_=c2_: e.activation(
                evb[:, 0 + (d % 2) * 1024:(d % 2) * 1024 + 1024],
                c2_[:, 1024:2048], AF.Sigmoid, bias=bcol("cb", d)),
                [c0] + bias_toks + sg_guard[d % 2])
            tsl = S.op("dve", lambda e, d=d, cb_=cb_, c2_=c2_: e.scalar_tensor_tensor(
                cb_[:, 2:1026], c2_[:, 0:1024], bcol("cb", d),
                evb[:, 2048 + (d % 2) * 1024:2048 + (d % 2) * 1024 + 1024],
                ALU.add, ALU.mult), [tsg, c0])
            tsl2 = S.op("dve", lambda e, d=d, cb_=cb_, c2_=c2_: e.scalar_tensor_tensor(
                cb_[:, 1026:2050], c2_[:, 1024:2048], bcol("cb", d),
                evb[:, (d % 2) * 1024:(d % 2) * 1024 + 1024],
                ALU.add, ALU.mult), [tsg2, tsl])
            sg_guard[d % 2] = [tsl, tsl2]
            tspl = S.dma("sp", sp_xps[d * 128:(d + 1) * 128, :], cb_[:, 2:2050],
                         [tsl, tsl2])
            t16x = S.op("dve", lambda e, d=d, cb_=cb_: e.tensor_copy(
                movb[:, (d % 2) * 2048:(d % 2) * 2048 + 2048], cb_[:, 2:2050]),
                [tsl, tsl2] + xp16_tok[d % 2])
            conv_guard[d % 2] = [tspl, t16x]
            xps = []
            for nb in range(4):
                xps.append(S.op("te", lambda e, d=d, nb=nb: e.matmul(
                    PB[0:96, nb * 512:(nb + 1) * 512],
                    wxS[:, d * 96:(d + 1) * 96],
                    movb[:, (d % 2) * 2048 + nb * 512:(d % 2) * 2048 + nb * 512 + 512],
                    start=(d == 0), stop=(d == NDSL - 1)),
                    [t16x] + wx_toks + xproj_mm[-1:]))
            xproj_mm += xps
            xp16_tok[d % 2] = [xps[-1]]
            dsl_done.append(tspl)
        else:
            d = m - NDSL
            zb = zbuf[d % 2]
            tzc = S.op("dve", lambda e, zb=zb: e.tensor_copy(zb[:, :], PA[:, 0:2048]),
                       mmt + zb_guard[d % 2])
            last_evac = [tzc]
            tsg = S.op("act", lambda e, zb=zb, d=d: e.activation(
                evb[:, 2048 + (d % 2) * 1024:2048 + (d % 2) * 1024 + 1024],
                zb[:, 0:1024], AF.Sigmoid), [tzc] + sg_guard[d % 2])
            tsg2 = S.op("act", lambda e, zb=zb, d=d: e.activation(
                evb[:, (d % 2) * 1024:(d % 2) * 1024 + 1024],
                zb[:, 1024:2048], AF.Sigmoid), [tzc] + sg_guard[d % 2])
            tsl = S.op("dve", lambda e, zb=zb, d=d: e.scalar_tensor_tensor(
                zb[:, 0:1024], zb[:, 0:1024], 0.0,
                evb[:, 2048 + (d % 2) * 1024:2048 + (d % 2) * 1024 + 1024],
                ALU.add, ALU.mult), [tsg])
            tsl2 = S.op("dve", lambda e, zb=zb, d=d: e.scalar_tensor_tensor(
                zb[:, 1024:2048], zb[:, 1024:2048], 0.0,
                evb[:, (d % 2) * 1024:(d % 2) * 1024 + 1024],
                ALU.add, ALU.mult), [tsg2])
            sg_guard[d % 2] = [tsl, tsl2]
            tspl = S.dma("sp", sp_zs[d * 128:(d + 1) * 128, :], zb[:, :], [tsl, tsl2])
            zb_guard[d % 2] = [tspl]
            dsl_done.append(tspl)
    tdbc1 = S.op("dve", lambda e: e.tensor_copy(dbc_dt[:, :], PB[0:64, :]), xproj_mm)
    tdbc2a = S.op("dve", lambda e: e.tensor_copy(movb[64:96, 0:2048], PB[64:96, :]),
                  [tdbc1] + last_evac + xp16_tok[0] + xp16_tok[1])
    tdbc2 = S.dma("sp", BCt[0:32, :], movb[64:96, 0:2048], [tdbc2a])
    twdt = S.dma("pool", wstat[0:64, 0:2048], wdtT[:, :], win_tok[0] + win_tok[1])
    dt_toks = []
    dt_ps = [[], []]
    for m in range(NDSL):
        par = m % 2
        ab_ = evb[:, 0:2048] if par == 0 else zbuf[0][:, :]
        sp_ = evb[:, 2048:4096] if par == 0 else zbuf[1][:, :]
        mmt = []
        for nb in range(4):
            mmt.append(S.op("te", lambda e, m=m, nb=nb: e.matmul(
                PA[:, nb * 512:(nb + 1) * 512],
                wstat[0:64, m * 128:(m + 1) * 128],
                dbc_dt[:, nb * 512:(nb + 1) * 512],
                start=True, stop=True),
                [twdt, tdbc1] + last_evac + sg_guard[0] + sg_guard[1] + zb_guard[0]
                + zb_guard[1] + dt_ps[0] + dt_ps[1]))
        t1 = S.op("act", lambda e, m=m, ab_=ab_: e.activation(
            ab_, PA[:, :], AF.Abs, bias=bcol("dtb", m)), mmt + dt_toks[-4:-2])
        t2 = S.op("act", lambda e, ab_=ab_: e.activation(ab_, ab_, AF.Exp,
                                                scale=-1.0), [t1])
        t3 = S.op("act", lambda e, ab_=ab_: e.activation(ab_, ab_, AF.Ln,
                                                bias=1.0), [t2])
        t4 = S.op("act", lambda e, m=m, sp_=sp_: e.activation(
            sp_, PA[:, :], AF.Relu, bias=bcol("dtb", m)), mmt + dt_toks[-4:-2])
        dt_ps[par] = [t1, t4]
        t5 = S.op("dve", lambda e, ab_=ab_, sp_=sp_: e.tensor_add(sp_, sp_, ab_),
                  [t3, t4])
        tspl = S.dma("sp", sp_dt[m * 128:(m + 1) * 128, :], sp_, [t5])
        dt_toks += [tspl, t5]
        last_evac = []
        sg_guard = [[], []]
        zb_guard = [[], []]
    p2_done = S.barrier_toks()

    S.phase("P3")
    prev_chunk = p2_done
    last_tp = []
    opmm_last = []
    op_prev = []
    htail = []
    rd_guard = [[], []]
    for ch in range(NCH):
        c0 = ch * LC
        bc_toks = []
        oh_tok = [[], []]
        for n in range(2 * NS):
            dst = (Bbc if n < NS else Cbc)[:, (n % NS) * LC:(n % NS) * LC + LC]
            osl = ohS[:, (n % 2) * 128:(n % 2) * 128 + 128]
            tohn = S.dma("sp", osl, ohb[:, n * 128:(n + 1) * 128],
                         oh_tok[n % 2] + (last_tp if ch else prev_chunk))
            tmms = []
            for half in range(2):
                pslc = PB[:, 1024 + half * 512:1024 + (half + 1) * 512]
                tmms.append(S.op("te", lambda e, osl=osl, pslc=pslc, half=half: e.matmul(
                    pslc, osl, BCt[:, c0 + half * 512:c0 + (half + 1) * 512],
                    start=True, stop=True),
                    [tohn, tdbc2] + bc_toks[-1:] + (last_tp + opmm_last if ch else prev_chunk)))
            oh_tok[n % 2] = [tmms[-1]]
            tev = S.op("act", lambda e, dst=dst: e.copy(dst, PB[:, 1024:2048]), tmms)
            bc_toks.append(tev)
        bcB = bc_toks[:NS]
        bcC = bc_toks[NS:]
        y3_toks = []
        op_toks = []
        for d in range(NDSL):
            par = d % 2
            dtb_ = rld[:, 0:LC] if par == 0 else evb[:, 2048:2048 + LC]
            xpb_ = rld[:, LC:2 * LC] if par == 0 else evb[:, 3072:3072 + LC]
            rdeps = dt_toks + dsl_done + rd_guard[par] + (opmm_last if ch else prev_chunk)
            tdt = S.dma("sp", dtb_, sp_dt[d * 128:(d + 1) * 128, c0:c0 + LC], rdeps)
            txp = S.dma("sp", xpb_, sp_xps[d * 128:(d + 1) * 128, c0:c0 + LC], rdeps)
            tzs = S.dma("sp", rld[:, 2 * LC:3 * LC], sp_zs[d * 128:(d + 1) * 128, c0:c0 + LC],
                        rdeps + (y3_toks[-1:] if y3_toks else []))
            tw = S.op("dve", lambda e, dtb_=dtb_, xpb_=xpb_: e.tensor_mul(
                wco[:, :], dtb_, xpb_),
                [tdt, txp] + (y3_toks[-1:] if y3_toks else (last_tp if ch else [])))
            htoks = []
            for g in range(NS // 2):
                sg = g % 2
                asl = a2[sg][:, :]
                prevp = htoks[-2:] if htoks else htail[-2:]
                etoks = []
                for jj in range(2):
                    n = 2 * g + jj
                    etoks.append(S.op("act", lambda e, asl=asl, jj=jj, d=d, n=n, dtb_=dtb_: e.activation(
                        asl[:, jj * LC:(jj + 1) * LC], dtb_, AF.Exp,
                        scale=Asb[:, d * NS + n:d * NS + n + 1]), [tdt, tA] + prevp))
                tu = S.op("dve", lambda e, g=g: e.tensor_tensor(
                    u2[:, :].rearrange("p (a b) -> p a b", a=2),
                    wco[:, None, :].broadcast_to([128, 2, LC]),
                    Bbc[:, 2 * g * LC:2 * (g + 1) * LC].rearrange("p (a b) -> p a b", a=2),
                    ALU.mult), [tw, bcB[2 * g + 1]] + prevp)
                for jj in range(2):
                    n = 2 * g + jj
                    hcol = hcar[:, d * NS + n:d * NS + n + 1]
                    hslc = hblk[:, n * LC:(n + 1) * LC]
                    tts = S.op("dve", lambda e, hslc=hslc, asl=asl, jj=jj, hcol=hcol:
                               e.tensor_tensor_scan(
                                   hslc, asl[:, jj * LC:(jj + 1) * LC],
                                   u2[:, jj * LC:(jj + 1) * LC], hcol,
                                   ALU.mult, ALU.add),
                               [tu, etoks[jj], thc])
                    htoks += [tts]
            tcx = S.op("dve", lambda e, d=d: e.tensor_copy(
                hcar[:, d * NS:(d + 1) * NS],
                hblk[:, :].rearrange("p (a b) -> p a b", a=NS)[:, :, LC - 1:LC]),
                htoks)
            tp = S.op("dve", lambda e: e.tensor_mul(hblk[:, :], hblk[:, :], Cbc[:, :]),
                      htoks + [tcx, bcC[-1]])
            tr = S.op("dve", lambda e: e.tensor_add(
                hblk[:, 0:8 * LC], hblk[:, 0:8 * LC], hblk[:, 8 * LC:16 * LC]), [tp])
            tr = S.op("dve", lambda e: e.tensor_add(
                hblk[:, 0:4 * LC], hblk[:, 0:4 * LC], hblk[:, 4 * LC:8 * LC]), [tr])
            tr = S.op("dve", lambda e: e.tensor_add(
                hblk[:, 0:2 * LC], hblk[:, 0:2 * LC], hblk[:, 2 * LC:4 * LC]), [tr])
            tr = S.op("dve", lambda e: e.tensor_add(
                ycol[:, :], hblk[:, 0:LC], hblk[:, LC:2 * LC]), [tr])
            ty2 = S.op("dve", lambda e, d=d, xpb_=xpb_: e.scalar_tensor_tensor(
                ycol[:, :], xpb_, bcol("Dp", d), ycol[:, :],
                ALU.mult, ALU.add), [tr, txp])
            ty3 = S.op("dve", lambda e, d=d: e.tensor_mul(
                y3slice(d), ycol[:, :], rld[:, 2 * LC:3 * LC]),
                [ty2, tzs] + (op_toks[-1:] if op_toks else opmm_last))
            y3_toks.append(ty3)
            rd_guard[par] = [ty3]
            if d == NDSL - 1:
                last_tp = [tp, ty3]
                htail = htoks[-2:]
        for m in range(NDT):
            wsl = wstat[:, (m % 2) * 2048:(m % 2) * 2048 + 2048]
            twl = S.dma("pool", wsl, woutQ[:, m * 2048:(m + 1) * 2048],
                        op_toks[-4:-2] if len(op_toks) >= 4 else (op_prev if ch else prev_chunk))
            pslc = (PA if m % 2 == 0 else PB)[:, 0:LC]
            mmt = []
            for k in range(NDSL):
                for nb in range(2):
                    mmt.append(S.op("te", lambda e, pslc=pslc, wsl=wsl, k=k, nb=nb: e.matmul(
                        pslc[:, nb * 512:(nb + 1) * 512], wsl[:, k * 128:(k + 1) * 128],
                        y3slice(k)[:, nb * 512:(nb + 1) * 512],
                        start=(k == 0), stop=(k == NDSL - 1)),
                        [twl] + y3_toks + (op_toks[-4:-3] if len(op_toks) >= 4 else (op_prev if ch else []))))
            txn = S.dma("sp", evb[:, 0:LC], sp_xn[m * 128:(m + 1) * 128, c0:c0 + LC],
                        xn_spill + (op_toks[-4:] if op_toks else op_prev))
            tx1 = S.op("dve", lambda e, pslc=pslc: e.tensor_add(
                evb[:, 1024:1024 + LC], pslc, evb[:, 0:LC]), mmt + [txn])
            tspl = S.dma("sp", sp_x1[m * 128:(m + 1) * 128, c0:c0 + LC],
                         evb[:, 1024:1024 + LC], [tx1])
            op_toks += [tx1, tspl]
        opmm_last = [op_toks[-2]]
        op_prev = op_toks[-4:]
        if ch == NCH - 1:
            prev_chunk = S.barrier_toks()

    S.phase("P4")
    x1_done = prev_chunk
    x2_sp = []
    wl_tok = [[], []]
    wl2_tok = [[], []]
    for ch in range(NCH):
        c0 = ch * LC
        x1_toks = []
        for m in range(NDT):
            tld = S.dma("sp", x1c[:, m * LC:(m + 1) * LC],
                        sp_x1[m * 128:(m + 1) * 128, c0:c0 + LC], x1_done + x2_sp[-2:])
            t16 = S.op("dve", lambda e, m=m: e.tensor_copy(
                x1bf[:, m * LC:(m + 1) * LC], x1c[:, m * LC:(m + 1) * LC]), [tld])
            x1_toks += [tld, t16]
        hf_toks = []
        for m in range(NFT):
            wsl = wstat[:, (m % 2) * 2048:(m % 2) * 2048 + NDT * 128]
            twl = S.dma("pool", wsl, w1Q[:, m * 1024:(m + 1) * 1024],
                        wl_tok[m % 2] + wl2_tok[m % 2] + x2_sp[-2:])
            pslc = (PA if m % 2 == 0 else PB)[:, 0:LC]
            mmt = []
            for k in range(NDT):
                for nb in range(2):
                    mmt.append(S.op("te", lambda e, pslc=pslc, wsl=wsl, k=k, nb=nb: e.matmul(
                        pslc[:, nb * 512:(nb + 1) * 512], wsl[:, k * 128:(k + 1) * 128],
                        x1bf[:, k * LC + nb * 512:k * LC + (nb + 1) * 512],
                        start=(k == 0), stop=(k == NDT - 1)),
                        [twl] + x1_toks + hf_toks[-2:-1]))
            wl_tok[m % 2] = [mmt[-1]]
            tev = S.op("act", lambda e, pslc=pslc, m=m: e.activation(
                hfb[:, m * LC:(m + 1) * LC], pslc, AF.Gelu, bias=bcol("b1", m)), mmt)
            hf_toks.append(tev)
        w2_toks = []
        for m in range(NDT):
            wsl = wstat[:, (m % 2) * 2048:(m % 2) * 2048 + 2048]
            msl = movb[:, (m % 2) * 2048:(m % 2) * 2048 + 2048]
            wdeps = wl2_tok[m % 2] + w2_toks[-2:] + wl_tok[0] + wl_tok[1]
            twl = S.dma("pool", wsl, w2Q[:, m * 4096:m * 4096 + 2048], wdeps)
            twl2 = S.dma("pool", msl, w2Q[:, m * 4096 + 2048:(m + 1) * 4096], wdeps)
            pslc = (PA if m % 2 == 0 else PB)[:, 0:LC]
            mmt = []
            for k in range(NFT):
                stat = wsl[:, k * 128:(k + 1) * 128] if k < 16 else \
                    msl[:, (k - 16) * 128:(k - 15) * 128]
                dep = twl if k < 16 else twl2
                for nb in range(2):
                    mmt.append(S.op("te", lambda e, pslc=pslc, stat=stat, k=k, nb=nb: e.matmul(
                        pslc[:, nb * 512:(nb + 1) * 512], stat,
                        hfb[:, k * LC + nb * 512:k * LC + (nb + 1) * 512],
                        start=(k == 0), stop=(k == NFT - 1)),
                        [dep] + hf_toks + (w2_toks[-4:-3] if len(w2_toks) >= 4 else [])))
            wl2_tok[m % 2] = [mmt[-1]]
            tx2 = S.op("dve", lambda e, pslc=pslc, m=m: e.scalar_tensor_tensor(
                evb[:, 0:LC], pslc, bcol("b2", m), x1c[:, m * LC:(m + 1) * LC],
                ALU.add, ALU.add), mmt + x1_toks)
            tspl = S.dma("sp", sp_x2[m * 128:(m + 1) * 128, c0:c0 + LC], evb[:, 0:LC],
                         [tx2])
            w2_toks += [tx2, tspl]
        x2_sp += w2_toks
        x1_done = S.barrier_toks()

    S.phase("P5")
    wo_toks = [S.dma("pool", woS[:, k * KB:(k + 1) * KB],
                     woT[k * 128:(k + 1) * 128, :], x1_done) for k in range(NDT)]
    par_guard = [[], []]
    p5b_tok = [[] for _ in range(4)]
    stat_guard = [[], []]
    prev_tffn = []
    prev_tsum = []
    fin = []
    for tt in range(NTOK):
        par = tt % 2
        r0 = tt * 128
        xb_t = xbt[par]
        tldx = S.dma("sp", xb_t[:, :], xb[r0:r0 + 128, :], par_guard[par])
        tln = ln_tile(xb_t[:, :], xnt5[par][:, :], 8 + 4 * par,
                      [tldx] + par_guard[par], scr=lnscr[par][:, :])
        p5banks = [PA[:, 1024:1024 + 128], PA[:, 1536:1536 + 128],
                   PB[:, 1024:1024 + 128], PB[:, 1536:1536 + 128]]
        st_toks = []
        for j in range(NDT):
            pslc = p5banks[j % 4]
            ttr = S.op("te", lambda e, p=pslc, j=j, par=par: e.transpose(
                p, xnt5[par][:, j * 128:(j + 1) * 128], ident[:, :]),
                [tln] + p5b_tok[j % 4])
            tev = S.op("act", lambda e, p=pslc, j=j, par=par: e.copy(
                xst5[:, par * KB + j * 128:par * KB + (j + 1) * 128], p),
                [ttr] + stat_guard[par])
            p5b_tok[j % 4] = [tev]
            st_toks.append(tev)
        mmt = []
        for k in range(NDT):
            for nb in range(2):
                mmt.append(S.op("te", lambda e, k=k, nb=nb, par=par: e.matmul(
                    PA[:, nb * 512:(nb + 1) * 512],
                    xst5[:, par * KB + k * 128:par * KB + (k + 1) * 128],
                    woS[:, k * KB + nb * 512:k * KB + (nb + 1) * 512],
                    start=(k == 0), stop=(k == NDT - 1)),
                    [st_toks[k]] + wo_toks + prev_tffn))
        stat_guard[par] = [mmt[-1]]
        tcu = S.dma("sp", cut[par][:, :], cc_out[r0:r0 + 128, :],
                    [tcoll] + par_guard[par])
        tffn = S.op("dve", lambda e, par=par: e.tensor_add(
            evb[:, par * 2048:par * 2048 + D], PA[:, 0:D], cut[par][:, :]),
            mmt + [tcu] + par_guard[par])
        prev_tffn = [tffn]
        ttanh = S.op("act", lambda e, par=par: e.activation(
            evb[:, par * 2048:par * 2048 + D], evb[:, par * 2048:par * 2048 + D],
            AF.Tanh), [tffn])
        tx2l = S.dma("sp", x2s[par][:, :].rearrange("p (j c) -> p j c", j=NDT),
                     sp_x2[:, r0:r0 + 128].rearrange("(j p) c -> p j c", p=128),
                     x2_sp + par_guard[par])
        tr2 = []
        for j in range(NDT):
            pslc = PB[:, j * 128:(j + 1) * 128]
            tr2.append(S.op("te", lambda e, p=pslc, j=j, par=par: e.transpose(
                p, x2s[par][:, j * 128:(j + 1) * 128], ident[:, :]),
                [tx2l] + prev_tsum))
        tsum = S.op("dve", lambda e, par=par: e.tensor_add(
            evb[:, par * 2048 + 1024:par * 2048 + 1024 + D], PB[:, 0:D],
            xbt[par][:, :]), tr2 + [tldx, tln] + par_guard[par])
        prev_tsum = [tsum]
        tout = S.op("dve", lambda e, par=par: e.tensor_add(
            evb[:, par * 2048 + 1024:par * 2048 + 1024 + D],
            evb[:, par * 2048 + 1024:par * 2048 + 1024 + D],
            evb[:, par * 2048:par * 2048 + D]), [tsum, ttanh])
        tst = S.dma("sp", out[r0:r0 + 128, :], evb[:, par * 2048 + 1024:par * 2048 + 1024 + D],
                    [tout])
        par_guard[par] = [tst, tout, mmt[-1], tr2[-1], tsum]
        fin = [tst]
    S.op("sp", lambda e: e.nop(), fin)

    S.emit(nc)
    return nc


def kernel(**inputs):
    f = np.float32
    x = np.ascontiguousarray(np.asarray(inputs["L0_attn_x"], f))

    def T(a):
        return np.ascontiguousarray(np.asarray(a, f).T)

    def col(a):
        return np.ascontiguousarray(np.asarray(a, f).reshape(-1, 1))

    def qform(wT, K, M):
        return np.ascontiguousarray(
            wT.reshape(K, 128, M, 128).transpose(1, 2, 0, 3).reshape(128, M * K * 128))

    import ml_dtypes
    onehot = np.zeros((32, 32 * 128), np.float32)
    for n in range(32):
        onehot[n, n * 128:(n + 1) * 128] = 1.0
    onehot = onehot.astype(ml_dtypes.bfloat16)

    common = {
        "winQ": qform(T(inputs["L0_attn_in_proj_w"]), NDT, 32),
        "cwc": np.ascontiguousarray(
            np.asarray(inputs["L0_attn_conv_w"], f).reshape(16, 128, 3)
            .transpose(1, 0, 2).reshape(128, 48)),
        "cbv": col(inputs["L0_attn_conv_b"]),
        "wxT": T(inputs["L0_attn_x_proj_w"]),
        "wdtT": T(inputs["L0_attn_dt_proj_w"]),
        "dtbv": col(inputs["L0_attn_dt_proj_b"]),
        "AsbH": np.ascontiguousarray(
            np.asarray(inputs["L0_attn_A_log"], f).reshape(16, 128, 16)
            .transpose(1, 0, 2).reshape(128, 256)),
        "Dpv": col(inputs["L0_attn_Dp"]),
        "woutQ": qform(T(inputs["L0_attn_out_proj_w"]), NDSL, NDT),
        "w1Q": qform(T(inputs["L0_attn_w1"]), NDT, NFT),
        "b1v": col(inputs["L0_attn_b1"]),
        "w2Q": qform(T(inputs["L0_attn_w2"]), NFT, NDT),
        "b2v": col(inputs["L0_attn_b2"]),
        "wfTl": T(inputs["L0_ffn_wf"]), "ufTl": T(inputs["L0_ffn_uf"]),
        "bfvl": col(inputs["L0_ffn_bf"]),
        "wiTl": T(inputs["L0_ffn_wi"]), "uiTl": T(inputs["L0_ffn_ui"]),
        "bivl": col(inputs["L0_ffn_bi"]),
        "wcTl": T(inputs["L0_ffn_wc"]), "ucTl": T(inputs["L0_ffn_uc"]),
        "bcvl": col(inputs["L0_ffn_bc"]),
        "woT": T(inputs["L0_ffn_wo"]), "uoT": T(inputs["L0_ffn_uo"]),
        "bov": col(inputs["L0_ffn_bo"]),
        "incv": col(inputs["L0_ffn_in_c"]),
        "ohb": onehot,
    }
    in_maps = []
    for b in range(NCORE):
        m = dict(common)
        m["xb"] = np.ascontiguousarray(x[b])
        m["xr"] = np.ascontiguousarray(x[:, b * RPC:(b + 1) * RPC, :].reshape(L, D))
        in_maps.append(m)

    nc = build_program()
    res = run_bass_kernel_spmd(nc, in_maps, core_ids=list(range(NCORE)))
    return np.stack([np.asarray(res.results[b]["out"], np.float32)
                     for b in range(NCORE)], axis=0)


# version 16
# speedup vs baseline: 1.0674x; 1.0082x over previous
import numpy as np
import concourse.bass as bass
import concourse.mybir as mybir
from concourse.bass_utils import run_bass_kernel_spmd

F32 = mybir.dt.float32
BF16 = mybir.dt.bfloat16
U8 = mybir.dt.uint8
AF = mybir.ActivationFunctionType
ALU = mybir.AluOpType
AXL = mybir.AxisListType

NCORE = 8
L = 2048
D = 1024
DI = 2048
NS = 16
RK = 64
FF = 4096
RPC = 256
NDSL = 16
NDT = 8
NFT = 32
NTOK = 16
LC = 1024
NCH = 2
EPS = 1e-5
KB = 1024


class Sched:
    ENGS = ("sp", "act", "dve", "pool", "te")
    KD = 12

    def __init__(self):
        self.phases = []
        self.cur = None
        self.xcnt = {e: 0 for e in self.ENGS}
        self.dcnt = {(e, j): 0 for e in self.ENGS for j in range(self.KD)}
        self.dseq = {e: 0 for e in self.ENGS}
        self.ccnt = 0

    def phase(self, name):
        self.cur = {e: [] for e in self.ENGS}
        self.cur["_name"] = name
        self.phases.append(self.cur)

    def op(self, eng, fn, deps=()):
        self.xcnt[eng] += 1
        self.cur[eng].append(("x", fn, tuple(deps), None))
        return (eng, "x", self.xcnt[eng])

    def dma(self, eng, out, in_, deps=()):
        j = self.dseq[eng] % self.KD
        self.dseq[eng] += 1
        self.dcnt[(eng, j)] += 1
        fn = lambda e, o=out, i=in_: e.dma_start(out=o, in_=i)
        self.cur[eng].append(("d", fn, tuple(deps), (j, self.dcnt[(eng, j)])))
        return (eng, ("d", j), self.dcnt[(eng, j)])

    def coll(self, fn, deps=()):
        self.ccnt += 1
        self.cur["pool"].append(("c", fn, tuple(deps), None))
        return ("pool", "c", self.ccnt)

    def barrier_toks(self):
        t = []
        for e in self.ENGS:
            if self.xcnt[e]:
                t.append((e, "x", self.xcnt[e]))
            for j in range(self.KD):
                if self.dcnt[(e, j)]:
                    t.append((e, ("d", j), self.dcnt[(e, j)]))
        if self.ccnt:
            t.append(("pool", "c", self.ccnt))
        return t

    def emit(self, nc):
        import contextlib
        with contextlib.ExitStack() as stack:
            sems = {}
            for e in self.ENGS:
                sems[(e, "x")] = stack.enter_context(nc.semaphore(f"sx_{e}"))
                for j in range(self.KD):
                    sems[(e, ("d", j))] = stack.enter_context(
                        nc.semaphore(f"sd_{e}_{j}"))
            sems[("pool", "c")] = stack.enter_context(nc.semaphore("sc_pool"))
            waited = {}

            def do_wait(eng, eo, skey, val):
                key = (eng, skey)
                if waited.get(key, 0) < val:
                    eo.wait_ge(sems[skey], val)
                    waited[key] = val

            def emit_engine(eng, ops, eo):
                for kind, fn, deps, dinfo in ops:
                    for dep in deps:
                        deng, dkind, dcount = dep
                        if dkind == "x":
                            val = dcount
                            skey = (deng, "x")
                        elif dkind == "c":
                            val = dcount
                            skey = ("pool", "c")
                        else:
                            val = 16 * dcount
                            skey = (deng, dkind)
                        do_wait(eng, eo, skey, val)
                    if kind == "d":
                        j, cnt = dinfo
                        if cnt > 1:
                            do_wait(eng, eo, (eng, ("d", j)), 16 * (cnt - 1))
                    inst = fn(eo)
                    if kind == "x":
                        inst.then_inc(sems[(eng, "x")], 1)
                    elif kind == "d":
                        j, cnt = dinfo
                        inst.then_inc(sems[(eng, ("d", j))], 16)
                    else:
                        inst.then_inc(sems[("pool", "c")], 1)

            for ph in self.phases:
                with nc.Block(ph["_name"], no_gpsimd_drain=True) as block:
                    decos = {"sp": block.sync, "act": block.scalar, "dve": block.vector,
                             "pool": block.gpsimd, "te": block.tensor}
                    for eng in self.ENGS:
                        if ph[eng]:
                            decos[eng](lambda eo, _o=ph[eng], _e=eng: emit_engine(_e, _o, eo))

def build_program():
    nc = bass.Bass("TRN2", target_bir_lowering=False)
    S = Sched()

    def din(name, shape, dt=F32):
        return nc.dram_tensor(name, shape, dt, kind="ExternalInput")

    xb = din("xb", [L, D])
    xr = din("xr", [L, D])
    winQ = din("winQ", [128, 32 * NDT * 128])
    cwc = din("cwc", [128, 3 * NDSL])
    cbv = din("cbv", [DI, 1])
    wxT = din("wxT", [DI, 96])
    wdtT = din("wdtT", [RK, DI])
    dtbv = din("dtbv", [DI, 1])
    AsbH = din("AsbH", [128, NDSL * NS])
    Dpv = din("Dpv", [DI, 1])
    woutQ = din("woutQ", [128, NDT * NDSL * 128])
    w1Q = din("w1Q", [128, NFT * NDT * 128])
    b1v = din("b1v", [FF, 1])
    w2Q = din("w2Q", [128, NDT * NFT * 128])
    b2v = din("b2v", [D, 1])
    wgT = {g: din(f"w{g}Tl", [D, D]) for g in "fic"}
    ugT = {g: din(f"u{g}Tl", [D, D]) for g in "fic"}
    bgv = {g: din(f"b{g}vl", [D, 1]) for g in "fic"}
    woT = din("woT", [D, D])
    uoT = din("uoT", [D, D])
    bov = din("bov", [D, 1])
    incv = din("incv", [D, 1])
    ohb = din("ohb", [32, 32 * 128], BF16)

    out = nc.dram_tensor("out", [L, D], F32, kind="ExternalOutput")

    sp_xps = nc.dram_tensor("sp_xps", [DI, L], F32)
    sp_zs = nc.dram_tensor("sp_zs", [DI, L], F32)
    sp_dt = nc.dram_tensor("sp_dt", [DI, L], F32)
    sp_xn = nc.dram_tensor("sp_xn", [D, L], F32)
    sp_x1 = nc.dram_tensor("sp_x1", [D, L], F32)
    sp_x2 = nc.dram_tensor("sp_x2", [D, L], F32)
    sp_gx = nc.dram_tensor("sp_gx", [3 * D, L], BF16)
    cc_in = nc.dram_tensor("cc_in", [RPC, D], F32)
    cc_out = nc.dram_tensor("cc_out", [L, D], F32)

    sbias = nc.alloc_sbuf_tensor("sbias", [128, 128], F32)
    Asb = nc.alloc_sbuf_tensor("Asb", [128, NDSL * NS], F32)
    cwS = nc.alloc_sbuf_tensor("cwS", [128, 3 * NDSL], F32)
    ident = nc.alloc_sbuf_tensor("ident", [128, 128], F32)
    smal = nc.alloc_sbuf_tensor("smal", [128, 256], F32)
    epsc = nc.alloc_sbuf_tensor("epsc", [128, 1], F32)
    nhlf = nc.alloc_sbuf_tensor("nhlf", [128, 1], F32)
    hcar = nc.alloc_sbuf_tensor("hcar", [128, NDSL * NS], F32)
    BCt = nc.alloc_sbuf_tensor("BCt", [32, L], BF16)
    dbc_dt = nc.alloc_sbuf_tensor("dbc_dt", [RK, L], BF16)
    ohS = nc.alloc_sbuf_tensor("ohS", [32, 2 * 128], BF16)
    evb = nc.alloc_sbuf_tensor("evb", [128, 4096], F32)
    wstat = nc.alloc_sbuf_tensor("wstat", [128, 4096], BF16)
    movb = nc.alloc_sbuf_tensor("movb", [128, 4096], BF16)

    AR_BYTES = 160 * KB
    arena = nc.alloc_sbuf_tensor("arena", [128, AR_BYTES], U8)
    abase = nc.lookup_mloc(arena).addr
    _n = [0]

    def at(shape, dt, off_kb):
        _n[0] += 1
        return nc.alloc_sbuf_tensor_at(f"v{_n[0]}", shape, dt,
                                       offset=abase + int(off_kb * KB))

    p0slot = [at([128, D], F32, 4 * i) for i in range(4)]
    cst = at([128, 8 * RPC], F32, 0)
    cbf = at([128, 8 * RPC], BF16, 8)
    gat = at([128, 3 * 8 * RPC], F32, 12)
    cuT = at([128, 8 * RPC], F32, 0)
    xnT = at([128, NDT * L], F32, 0)
    a2 = [at([128, 2048], F32, 0), at([128, 2048], F32, 8)]
    rld = at([128, 3 * LC], F32, 16)
    ycol = at([128, LC], F32, 28)
    wco = at([128, LC], BF16, 32)
    u2 = at([128, 2048], BF16, 34)
    y3a = at([128, 13 * LC], BF16, 38)
    x1c = at([128, NDT * LC], F32, 0)
    x1bf = at([128, NDT * LC], BF16, 32)
    xbt = [at([128, D], F32, 0), at([128, D], F32, 16)]
    cut = [at([128, D], F32, 4), at([128, D], F32, 20)]
    x2s = [at([128, D], F32, 8), at([128, D], F32, 24)]
    xnt5 = [at([128, D], F32, 12), at([128, D], F32, 28)]
    lnscr = [at([128, D], F32, 32), at([128, D], F32, 36)]

    lstmW = at([128, 3 * NDT * KB], BF16, 64)
    xnrT = at([128, NDT * L], BF16, 112)
    gxstg = [at([128, 2048], BF16, 144 + 4 * i) for i in range(2)]
    uos = at([128, NDT * KB], BF16, 144)
    gxld = [at([128, 3 * NDT * RPC], BF16, 112 + 12 * i) for i in range(2)]
    xn16 = at([128, NDT * L], BF16, 64)
    convb = at([128, 2056], F32, 96)
    conv2 = at([128, 2048], F32, 104.5)
    wxS = at([128, NDSL * 96], BF16, 113)
    convb2 = at([128, 2056], F32, 117)
    conv2B = at([128, 2048], F32, 125.5)
    zbuf = [at([128, 2048], F32, 134), at([128, 2048], F32, 142)]
    Bbc = at([128, NS * LC], BF16, 64)
    Cbc = at([128, NS * LC], BF16, 96)
    hblk = at([128, NS * LC], BF16, 128)
    hfb = at([128, NFT * LC], BF16, 64)
    woS = at([128, NDT * KB], BF16, 64)
    xst5 = at([128, 2 * KB], BF16, 80)

    def y3slice(d):
        if d < 13:
            return y3a[:, d * LC:(d + 1) * LC]
        return movb[:, (d - 13) * LC:(d - 12) * LC]

    PA = nc.alloc_psum_tensor("PA", [128, 2048], F32)
    PB = nc.alloc_psum_tensor("PB", [128, 2048], F32)

    bias_cols = {}

    def load_bias(name, dram, n_tiles, col0):
        toks = []
        for i in range(n_tiles):
            toks.append(S.dma("sp", sbias[:, col0 + i:col0 + i + 1],
                              dram[i * 128:(i + 1) * 128, 0:1]))
        bias_cols[name] = col0
        return toks

    def bcol(name, i):
        c = bias_cols[name] + i
        return sbias[:, c:c + 1]

    teps = None

    def ln_tile(src, dst, so, deps, scr=None):
        if scr is None:
            scr = evb[:, 3072:4096]
        mcol = smal[:, so:so + 1]
        sqcol = smal[:, so + 1:so + 2]
        vcol = smal[:, so + 2:so + 3]
        rcol = smal[:, so + 3:so + 4]
        t1 = S.op("dve", lambda e: e.reduce_sum(mcol, src, AXL.X), deps)
        t1 = S.op("dve", lambda e: e.tensor_scalar(mcol, mcol, 1.0 / D, None, ALU.mult), [t1])
        t2 = S.op("act", lambda e: e.activation(scr, src, AF.Square,
                                                accum_out=sqcol), deps)
        t3 = S.op("dve", lambda e: e.scalar_tensor_tensor(vcol, mcol, -1.0, mcol,
                                                          ALU.mult, ALU.mult), [t1])
        t4 = S.op("dve", lambda e: e.scalar_tensor_tensor(vcol, sqcol, 1.0 / D, vcol,
                                                          ALU.mult, ALU.add), [t2, t3])
        t5 = S.op("act", lambda e: e.activation(rcol, vcol, AF.Ln, bias=epsc[:, :]),
                  [t4, teps])
        t6 = S.op("act", lambda e: e.activation(rcol, rcol, AF.Exp, scale=nhlf[:, :]),
                  [t5])
        t7 = S.op("dve", lambda e: e.tensor_scalar(dst, src, mcol, rcol,
                                                   ALU.subtract, ALU.mult), [t6])
        return t7

    S.phase("P0")
    tid = S.op("pool", lambda e: e.memset(ident[:, :], 0.0))
    tid = S.op("pool", lambda e: e.affine_select(
        out=ident[:, :], in_=ident[:, :], compare_op=ALU.not_equal,
        fill=1.0, base=0, pattern=[[-1, 128]], channel_multiplier=1), [tid])
    thc = S.op("pool", lambda e: e.memset(hcar[:, :], 0.0))
    teps = S.op("pool", lambda e: e.memset(epsc[:, :], EPS))
    teps = S.op("pool", lambda e: e.memset(nhlf[:, :], -0.5), [teps])
    bias_toks = []
    bias_toks += load_bias("cb", cbv, NDSL, 0)
    bias_toks += load_bias("dtb", dtbv, NDSL, 16)
    bias_toks += load_bias("Dp", Dpv, NDSL, 32)
    bias_toks += load_bias("b1", b1v, NFT, 48)
    bias_toks += load_bias("b2", b2v, NDT, 80)
    bias_toks += load_bias("bf", bgv["f"], NDT, 88)
    bias_toks += load_bias("bi", bgv["i"], NDT, 96)
    bias_toks += load_bias("bc", bgv["c"], NDT, 104)
    bias_toks += load_bias("bo", bov, NDT, 112)
    bias_toks += load_bias("inc", incv, NDT, 120)
    tA = S.dma("sp", Asb[:, :], AsbH[:, :])
    tA = S.op("act", lambda e: e.activation(Asb[:, :], Asb[:, :], AF.Exp), [tA])
    tA = S.op("dve", lambda e: e.tensor_scalar(Asb[:, :], Asb[:, :], -1.0, None,
                                               ALU.mult), [tA])
    tcw = S.dma("sp", cwS[:, :], cwc[:, :])

    slot_tok = [[] for _ in range(4)]
    ps_tok = [[] for _ in range(NDT)]
    for tt in range(NTOK):
        s = tt % 4
        src = p0slot[s]
        tld = S.dma("sp", src[:, :], xr[tt * 128:(tt + 1) * 128, :], slot_tok[s])
        tln = ln_tile(src[:, :], src[:, :], 4 * s, [tld])
        evs = [tln]
        trs = []
        for j in range(NDT):
            pslc = (PA if j < 4 else PB)[:, (j % 4) * 512:(j % 4) * 512 + 128]
            trs.append(S.op("te", lambda e, p=pslc, src=src, j=j: e.transpose(
                p, src[:, j * 128:(j + 1) * 128], ident[:, :]),
                [tln, tid] + ps_tok[0 if j < 4 else 1]))
        for grp in range(2):
            psb = PA if grp == 0 else PB
            pview = psb[:, :].rearrange("p (a b) -> p a b", a=4)[:, :, 0:128]
            oview = xnrT[:, :].rearrange("p (j l) -> p j l", j=NDT)[
                :, grp * 4:(grp + 1) * 4, tt * 128:(tt + 1) * 128]
            tev = S.op("act", lambda e, pview=pview, oview=oview: e.copy(
                oview, pview), [trs[grp * 4 + 3]])
            ps_tok[grp] = [tev]
            evs.append(tev)
        slot_tok[s] = evs
    xnr_done = S.barrier_toks()

    S.phase("P1a")
    wg_toks = []
    for gi, g in enumerate("fic"):
        for k in range(NDT):
            wg_toks.append(S.dma("pool",
                                 lstmW[:, (gi * NDT + k) * KB:(gi * NDT + k + 1) * KB],
                                 wgT[g][k * 128:(k + 1) * 128, :]))
    gx_sp = []
    stg_tok = [[], []]
    for gi in range(3):
        for m in range(NDT):
            ps = PA if (m % 2 == 0) else PB
            mmt = []
            for k in range(NDT):
                stat = lstmW[:, (gi * NDT + k) * KB + m * 128:(gi * NDT + k) * KB + m * 128 + 128]
                for nb in range(4):
                    mmt.append(S.op("te", lambda e, ps=ps, stat=stat, k=k, nb=nb: e.matmul(
                        ps[:, nb * 512:(nb + 1) * 512], stat,
                        xnrT[:, k * L + nb * 512: k * L + (nb + 1) * 512],
                        start=(k == 0), stop=(k == NDT - 1)),
                        wg_toks + xnr_done + (gx_sp[-2:] if len(gx_sp) >= 2 else [])))
            sg = (gi * NDT + m) % 2
            tev = S.op("dve", lambda e, ps=ps, sg=sg: e.tensor_copy(
                gxstg[sg][:, :], ps[:, :]), mmt + stg_tok[sg])
            tsp = S.dma("sp", sp_gx[(gi * NDT + m) * 128:(gi * NDT + m + 1) * 128, :],
                        gxstg[sg][:, :], [tev])
            stg_tok[sg] = [tsp]
            gx_sp.append(tsp)
    gx_done = S.barrier_toks()

    S.phase("P1b")
    ug_toks = []
    for gi, g in enumerate("fic"):
        for k in range(NDT):
            ug_toks.append(S.dma("pool",
                                 lstmW[:, (gi * NDT + k) * KB:(gi * NDT + k + 1) * KB],
                                 ugT[g][k * 128:(k + 1) * 128, :], gx_done))
    uo_toks = [S.dma("pool", uos[:, k * KB:(k + 1) * KB],
                     uoT[k * 128:(k + 1) * 128, :], gx_done) for k in range(NDT)]

    cinit = []
    for m in range(NDT):
        sl = slice(m * RPC, (m + 1) * RPC)
        t0 = S.op("dve", lambda e, sl=sl: e.memset(cst[:, sl], 0.0), gx_done)
        t1 = S.op("dve", lambda e, sl=sl, m=m: e.tensor_scalar(
            cst[:, sl], cst[:, sl], bcol("inc", m), None, ALU.add), [t0] + bias_toks)
        t2 = S.op("dve", lambda e, sl=sl: e.tensor_copy(cbf[:, sl], cst[:, sl]), [t1])
        cinit.append(t2)

    cprev = cinit
    gxld_tok = [[], []]
    gslot = {}
    preslot = {}
    for t in range(8):
        sg = t % 2
        tgx = S.dma("sp", gxld[sg][:, :].rearrange("p (q c) -> p q c", q=24),
                    sp_gx[:, t * RPC:(t + 1) * RPC].rearrange("(q p) c -> p q c", p=128),
                    gx_sp + gxld_tok[sg])
        gate_toks = {}
        for gi, g in enumerate("fic"):
            gt = []
            for m in range(NDT):
                ps = PA if (m % 2 == 0) else PB
                pslc = ps[:, ((m // 2) % 2) * 512: ((m // 2) % 2) * 512 + RPC]
                skey = (m % 2, (m // 2) % 2)
                mmt = []
                for k in range(NDT):
                    stat = lstmW[:, (gi * NDT + k) * KB + m * 128:(gi * NDT + k) * KB + m * 128 + 128]
                    mmt.append(S.op("te", lambda e, p=pslc, stat=stat, k=k: e.matmul(
                        p, stat, cbf[:, k * RPC:(k + 1) * RPC],
                        start=(k == 0), stop=(k == NDT - 1)),
                        ug_toks + cprev + gslot.get(skey, [])))
                pre = evb[:, ((gi * NDT + m) % 4) * RPC:((gi * NDT + m) % 4) * RPC + RPC]
                gxsl = gxld[sg][:, (gi * NDT + m) * RPC:(gi * NDT + m + 1) * RPC]
                tadd = S.op("dve", lambda e, p=pslc, pre=pre, gxsl=gxsl: e.tensor_add(
                    pre, p, gxsl), mmt + [tgx] + preslot.get((gi * NDT + m) % 4, []))
                gslot[skey] = [tadd]
                fn = AF.Sigmoid if g in "fi" else AF.Tanh
                bn = {"f": "bf", "i": "bi", "c": "bc"}[g]
                tact = S.op("act", lambda e, pre=pre, gi=gi, m=m, fn=fn, bn=bn: e.activation(
                    gat[:, gi * 8 * RPC + m * RPC: gi * 8 * RPC + (m + 1) * RPC], pre,
                    fn, bias=bcol(bn, m)), [tadd])
                preslot[(gi * NDT + m) % 4] = [tact]
                gt.append(tact)
            gate_toks[g] = gt
        gxld_tok[sg] = [gate_toks["c"][-1]]
        cnew = []
        for m in range(NDT):
            sl = slice(m * RPC, (m + 1) * RPC)
            f_ = gat[:, 0 * 8 * RPC + m * RPC: 0 * 8 * RPC + (m + 1) * RPC]
            i_ = gat[:, 1 * 8 * RPC + m * RPC: 1 * 8 * RPC + (m + 1) * RPC]
            g_ = gat[:, 2 * 8 * RPC + m * RPC: 2 * 8 * RPC + (m + 1) * RPC]
            deps = [gate_toks["f"][m], gate_toks["i"][m], gate_toks["c"][m]]
            t1 = S.op("dve", lambda e, f_=f_, sl=sl: e.tensor_mul(
                evb[:, 1024:1280], f_, cst[:, sl]), deps)
            t2 = S.op("dve", lambda e, i_=i_, g_=g_: e.tensor_mul(
                evb[:, 1280:1536], i_, g_), deps)
            t3 = S.op("dve", lambda e, sl=sl: e.tensor_add(
                cst[:, sl], evb[:, 1024:1280], evb[:, 1280:1536]), [t1, t2])
            t4 = S.op("dve", lambda e, sl=sl: e.tensor_copy(cbf[:, sl], cst[:, sl]), [t3])
            cnew.append(t4)
        cprev = cnew

    cu_toks = []
    for m in range(NDT):
        pslc = (PA if m % 2 == 0 else PB)[:, 1024:1024 + RPC]
        mmt = []
        for k in range(NDT):
            mmt.append(S.op("te", lambda e, p=pslc, k=k, m=m: e.matmul(
                p, uos[:, k * KB + m * 128: k * KB + m * 128 + 128],
                cbf[:, k * RPC:(k + 1) * RPC],
                start=(k == 0), stop=(k == NDT - 1)), uo_toks + cprev + cu_toks[-1:]))
        tev = S.op("dve", lambda e, p=pslc, m=m: e.tensor_scalar(
            cuT[:, m * RPC:(m + 1) * RPC], p, bcol("bo", m), None, ALU.add), mmt)
        cu_toks.append(tev)

    trtoks = []
    cu_ps = [[] for _ in range(4)]
    for half in range(2):
        evtoks = []
        for j in range(NDT):
            pslc = PB[:, (j % 4) * 512:(j % 4) * 512 + 128]
            ttr = S.op("te", lambda e, p=pslc, j=j, half=half: e.transpose(
                p, cuT[:, j * RPC + half * 128: j * RPC + half * 128 + 128], ident[:, :]),
                cu_toks + trtoks + cu_ps[j % 4])
            tev = S.op("dve", lambda e, p=pslc, j=j: e.tensor_copy(
                evb[:, j * 128:(j + 1) * 128], p), [ttr])
            cu_ps[j % 4] = [tev]
            evtoks.append(tev)
        trtoks.append(S.dma("sp", cc_in[half * 128:(half + 1) * 128, :],
                            evb[:, 0:D], evtoks))
    tcoll = S.coll(lambda e: e.collective_compute(
        "AllGather", ALU.bypass, replica_groups=[list(range(NCORE))],
        ins=[cc_in.ap().opt()], outs=[cc_out.ap().opt()]), trtoks)

    S.phase("P2")
    slot_tok = [[] for _ in range(2)]
    ps_tok = [[] for _ in range(NDT)]
    for tt in range(NTOK):
        s = tt % 2
        src = evb[:, s * 1024:(s + 1) * 1024]
        tld = S.dma("sp", src, xb[tt * 128:(tt + 1) * 128, :], slot_tok[s])
        tln = ln_tile(src, src, 4 * s, [tld])
        evs = [tln]
        trs = []
        for j in range(NDT):
            pslc = (PA if j < 4 else PB)[:, (j % 4) * 512:(j % 4) * 512 + 128]
            trs.append(S.op("te", lambda e, p=pslc, src=src, j=j: e.transpose(
                p, src[:, j * 128:(j + 1) * 128], ident[:, :]),
                [tln] + ps_tok[0 if j < 4 else 1]))
        for grp in range(2):
            psb = PA if grp == 0 else PB
            pview = psb[:, :].rearrange("p (a b) -> p a b", a=4)[:, :, 0:128]
            oview = xnT[:, :].rearrange("p (j l) -> p j l", j=NDT)[
                :, grp * 4:(grp + 1) * 4, tt * 128:(tt + 1) * 128]
            tev = S.op("dve", lambda e, pview=pview, oview=oview: e.tensor_copy(
                oview, pview), [trs[grp * 4 + 3]])
            ps_tok[grp] = [tev]
            evs.append(tev)
        slot_tok[s] = evs
    ln_all = S.barrier_toks()
    xn_spill = [S.dma("sp", sp_xn[j * 128:(j + 1) * 128, :], xnT[:, j * L:(j + 1) * L],
                      ln_all) for j in range(NDT)]
    xn16_toks = [S.op("dve", lambda e, k=k: e.tensor_copy(
        xn16[:, k * L:(k + 1) * L], xnT[:, k * L:(k + 1) * L]), ln_all)
        for k in range(NDT)]
    wx_toks = [S.dma("pool", wxS[:, k * 96:(k + 1) * 96],
                     wxT[k * 128:(k + 1) * 128, :], xnr_done) for k in range(NDSL)]

    xp16_tok = [[], []]
    xproj_mm = []
    dsl_done = []
    last_evac = []
    win_tok = [[], []]
    conv_guard = [[], []]
    sg_guard = [[], []]
    zb_guard = [[], []]
    for m in range(2 * NDSL):
        wsl = wstat[:, (m % 2) * 2048:(m % 2) * 2048 + NDT * 128]
        twl = S.dma("pool", wsl, winQ[:, m * 1024:(m + 1) * 1024],
                    win_tok[m % 2] + ([tcoll] if m < 2 else []))
        mmt = []
        for k in range(NDT):
            stat = wsl[:, k * 128:(k + 1) * 128]
            for nb in range(4):
                mmt.append(S.op("te", lambda e, stat=stat, k=k, nb=nb: e.matmul(
                    PA[:, nb * 512:(nb + 1) * 512], stat,
                    xn16[:, k * L + nb * 512: k * L + (nb + 1) * 512],
                    start=(k == 0), stop=(k == NDT - 1)),
                    [twl, xn16_toks[k]] + last_evac))
        win_tok[m % 2] = [mmt[-1]]
        if m < NDSL:
            d = m
            cb_ = convb if d % 2 == 0 else convb2
            c2_ = conv2 if d % 2 == 0 else conv2B
            tz = S.op("dve", lambda e, cb_=cb_: e.memset(cb_[:, 0:2], 0.0),
                      conv_guard[d % 2])
            tev = S.op("dve", lambda e, cb_=cb_: e.tensor_copy(cb_[:, 2:2050], PA[:, :]),
                       mmt + conv_guard[d % 2])
            last_evac = [tev]
            c2 = S.op("dve", lambda e, d=d, cb_=cb_, c2_=c2_: e.tensor_scalar(
                c2_[:, :], cb_[:, 2:2050], cwS[:, 3 * d + 2:3 * d + 3], None,
                ALU.mult), [tev, tcw])
            c1 = S.op("dve", lambda e, d=d, cb_=cb_, c2_=c2_: e.scalar_tensor_tensor(
                c2_[:, :], cb_[:, 1:2049], cwS[:, 3 * d + 1:3 * d + 2], c2_[:, :],
                ALU.mult, ALU.add), [c2, tz])
            c0 = S.op("dve", lambda e, d=d, cb_=cb_, c2_=c2_: e.scalar_tensor_tensor(
                c2_[:, :], cb_[:, 0:2048], cwS[:, 3 * d + 0:3 * d + 1], c2_[:, :],
                ALU.mult, ALU.add), [c1])
            tsg = S.op("act", lambda e, d=d, c2_=c2_: e.activation(
                evb[:, 2048 + (d % 2) * 1024:2048 + (d % 2) * 1024 + 1024],
                c2_[:, 0:1024], AF.Sigmoid, bias=bcol("cb", d)),
                [c0] + bias_toks + sg_guard[d % 2])
            tsg2 = S.op("act", lambda e, d=d, c2_=c2_: e.activation(
                evb[:, 0 + (d % 2) * 1024:(d % 2) * 1024 + 1024],
                c2_[:, 1024:2048], AF.Sigmoid, bias=bcol("cb", d)),
                [c0] + bias_toks + sg_guard[d % 2])
            tsl = S.op("dve", lambda e, d=d, cb_=cb_, c2_=c2_: e.scalar_tensor_tensor(
                cb_[:, 2:1026], c2_[:, 0:1024], bcol("cb", d),
                evb[:, 2048 + (d % 2) * 1024:2048 + (d % 2) * 1024 + 1024],
                ALU.add, ALU.mult), [tsg, c0])
            tsl2 = S.op("dve", lambda e, d=d, cb_=cb_, c2_=c2_: e.scalar_tensor_tensor(
                cb_[:, 1026:2050], c2_[:, 1024:2048], bcol("cb", d),
                evb[:, (d % 2) * 1024:(d % 2) * 1024 + 1024],
                ALU.add, ALU.mult), [tsg2, tsl])
            sg_guard[d % 2] = [tsl, tsl2]
            tspl = S.dma("sp", sp_xps[d * 128:(d + 1) * 128, :], cb_[:, 2:2050],
                         [tsl, tsl2])
            t16x = S.op("dve", lambda e, d=d, cb_=cb_: e.tensor_copy(
                movb[:, (d % 2) * 2048:(d % 2) * 2048 + 2048], cb_[:, 2:2050]),
                [tsl, tsl2] + xp16_tok[d % 2])
            conv_guard[d % 2] = [tspl, t16x]
            xps = []
            for nb in range(4):
                xps.append(S.op("te", lambda e, d=d, nb=nb: e.matmul(
                    PB[0:96, nb * 512:(nb + 1) * 512],
                    wxS[:, d * 96:(d + 1) * 96],
                    movb[:, (d % 2) * 2048 + nb * 512:(d % 2) * 2048 + nb * 512 + 512],
                    start=(d == 0), stop=(d == NDSL - 1)),
                    [t16x] + wx_toks + xproj_mm[-1:]))
            xproj_mm += xps
            xp16_tok[d % 2] = [xps[-1]]
            dsl_done.append(tspl)
        else:
            d = m - NDSL
            zb = zbuf[d % 2]
            tzc = S.op("dve", lambda e, zb=zb: e.tensor_copy(zb[:, :], PA[:, 0:2048]),
                       mmt + zb_guard[d % 2])
            last_evac = [tzc]
            tsg = S.op("act", lambda e, zb=zb, d=d: e.activation(
                evb[:, 2048 + (d % 2) * 1024:2048 + (d % 2) * 1024 + 1024],
                zb[:, 0:1024], AF.Sigmoid), [tzc] + sg_guard[d % 2])
            tsg2 = S.op("act", lambda e, zb=zb, d=d: e.activation(
                evb[:, (d % 2) * 1024:(d % 2) * 1024 + 1024],
                zb[:, 1024:2048], AF.Sigmoid), [tzc] + sg_guard[d % 2])
            tsl = S.op("dve", lambda e, zb=zb, d=d: e.scalar_tensor_tensor(
                zb[:, 0:1024], zb[:, 0:1024], 0.0,
                evb[:, 2048 + (d % 2) * 1024:2048 + (d % 2) * 1024 + 1024],
                ALU.add, ALU.mult), [tsg])
            tsl2 = S.op("dve", lambda e, zb=zb, d=d: e.scalar_tensor_tensor(
                zb[:, 1024:2048], zb[:, 1024:2048], 0.0,
                evb[:, (d % 2) * 1024:(d % 2) * 1024 + 1024],
                ALU.add, ALU.mult), [tsg2])
            sg_guard[d % 2] = [tsl, tsl2]
            tspl = S.dma("sp", sp_zs[d * 128:(d + 1) * 128, :], zb[:, :], [tsl, tsl2])
            zb_guard[d % 2] = [tspl]
            dsl_done.append(tspl)
    tdbc1 = S.op("dve", lambda e: e.tensor_copy(dbc_dt[:, :], PB[0:64, :]), xproj_mm)
    tdbc2a = S.op("dve", lambda e: e.tensor_copy(movb[64:96, 0:2048], PB[64:96, :]),
                  [tdbc1] + last_evac + xp16_tok[0] + xp16_tok[1])
    tdbc2 = S.dma("sp", BCt[0:32, :], movb[64:96, 0:2048], [tdbc2a])
    twdt = S.dma("pool", wstat[0:64, 0:2048], wdtT[:, :], win_tok[0] + win_tok[1])
    dt_toks = []
    dt_ps = [[], []]
    for m in range(NDSL):
        par = m % 2
        ab_ = evb[:, 0:2048] if par == 0 else zbuf[0][:, :]
        sp_ = evb[:, 2048:4096] if par == 0 else zbuf[1][:, :]
        mmt = []
        for nb in range(4):
            mmt.append(S.op("te", lambda e, m=m, nb=nb: e.matmul(
                PA[:, nb * 512:(nb + 1) * 512],
                wstat[0:64, m * 128:(m + 1) * 128],
                dbc_dt[:, nb * 512:(nb + 1) * 512],
                start=True, stop=True),
                [twdt, tdbc1] + last_evac + sg_guard[0] + sg_guard[1] + zb_guard[0]
                + zb_guard[1] + dt_ps[0] + dt_ps[1]))
        t1 = S.op("act", lambda e, m=m, ab_=ab_: e.activation(
            ab_, PA[:, :], AF.Abs, bias=bcol("dtb", m)), mmt + dt_toks[-4:-2])
        t2 = S.op("act", lambda e, ab_=ab_: e.activation(ab_, ab_, AF.Exp,
                                                scale=-1.0), [t1])
        t3 = S.op("act", lambda e, ab_=ab_: e.activation(ab_, ab_, AF.Ln,
                                                bias=1.0), [t2])
        t4 = S.op("act", lambda e, m=m, sp_=sp_: e.activation(
            sp_, PA[:, :], AF.Relu, bias=bcol("dtb", m)), mmt + dt_toks[-4:-2])
        dt_ps[par] = [t1, t4]
        t5 = S.op("dve", lambda e, ab_=ab_, sp_=sp_: e.tensor_add(sp_, sp_, ab_),
                  [t3, t4])
        tspl = S.dma("sp", sp_dt[m * 128:(m + 1) * 128, :], sp_, [t5])
        dt_toks += [tspl, t5]
        last_evac = []
        sg_guard = [[], []]
        zb_guard = [[], []]
    p2_done = S.barrier_toks()

    S.phase("P3")
    prev_chunk = p2_done
    for ch in range(NCH):
        c0 = ch * LC
        bc_toks = []
        oh_tok = [[], []]
        for n in range(2 * NS):
            dst = (Bbc if n < NS else Cbc)[:, (n % NS) * LC:(n % NS) * LC + LC]
            osl = ohS[:, (n % 2) * 128:(n % 2) * 128 + 128]
            tohn = S.dma("sp", osl, ohb[:, n * 128:(n + 1) * 128],
                         oh_tok[n % 2] + prev_chunk)
            tmms = []
            for half in range(2):
                pslc = PB[:, 1024 + half * 512:1024 + (half + 1) * 512]
                tmms.append(S.op("te", lambda e, osl=osl, pslc=pslc, half=half: e.matmul(
                    pslc, osl, BCt[:, c0 + half * 512:c0 + (half + 1) * 512],
                    start=True, stop=True),
                    [tohn, tdbc2] + bc_toks[-1:] + prev_chunk))
            oh_tok[n % 2] = [tmms[-1]]
            tev = S.op("act", lambda e, dst=dst: e.copy(dst, PB[:, 1024:2048]), tmms)
            bc_toks.append(tev)
        bcB = bc_toks[:NS]
        bcC = bc_toks[NS:]
        y3_toks = []
        op_toks = []
        rd_guard = [[], []]
        for d in range(NDSL):
            par = d % 2
            dtb_ = rld[:, 0:LC] if par == 0 else evb[:, 2048:2048 + LC]
            xpb_ = rld[:, LC:2 * LC] if par == 0 else evb[:, 3072:3072 + LC]
            rdeps = dt_toks + dsl_done + rd_guard[par] + prev_chunk
            tdt = S.dma("sp", dtb_, sp_dt[d * 128:(d + 1) * 128, c0:c0 + LC], rdeps)
            txp = S.dma("sp", xpb_, sp_xps[d * 128:(d + 1) * 128, c0:c0 + LC], rdeps)
            tzs = S.dma("sp", rld[:, 2 * LC:3 * LC], sp_zs[d * 128:(d + 1) * 128, c0:c0 + LC],
                        rdeps + (y3_toks[-1:] if y3_toks else []))
            tw = S.op("dve", lambda e, dtb_=dtb_, xpb_=xpb_: e.tensor_mul(
                wco[:, :], dtb_, xpb_), [tdt, txp] + (y3_toks[-1:] if y3_toks else []))
            htoks = []
            for g in range(NS // 2):
                sg = g % 2
                asl = a2[sg][:, :]
                prevp = htoks[-2:] if htoks else []
                etoks = []
                for jj in range(2):
                    n = 2 * g + jj
                    etoks.append(S.op("act", lambda e, asl=asl, jj=jj, d=d, n=n, dtb_=dtb_: e.activation(
                        asl[:, jj * LC:(jj + 1) * LC], dtb_, AF.Exp,
                        scale=Asb[:, d * NS + n:d * NS + n + 1]), [tdt, tA] + prevp))
                tu = S.op("dve", lambda e, g=g: e.tensor_tensor(
                    u2[:, :].rearrange("p (a b) -> p a b", a=2),
                    wco[:, None, :].broadcast_to([128, 2, LC]),
                    Bbc[:, 2 * g * LC:2 * (g + 1) * LC].rearrange("p (a b) -> p a b", a=2),
                    ALU.mult), [tw, bcB[2 * g + 1]] + prevp)
                for jj in range(2):
                    n = 2 * g + jj
                    hcol = hcar[:, d * NS + n:d * NS + n + 1]
                    hslc = hblk[:, n * LC:(n + 1) * LC]
                    tts = S.op("dve", lambda e, hslc=hslc, asl=asl, jj=jj, hcol=hcol:
                               e.tensor_tensor_scan(
                                   hslc, asl[:, jj * LC:(jj + 1) * LC],
                                   u2[:, jj * LC:(jj + 1) * LC], hcol,
                                   ALU.mult, ALU.add),
                               [tu, etoks[jj], thc])
                    htoks += [tts]
            tcx = S.op("dve", lambda e, d=d: e.tensor_copy(
                hcar[:, d * NS:(d + 1) * NS],
                hblk[:, :].rearrange("p (a b) -> p a b", a=NS)[:, :, LC - 1:LC]),
                htoks)
            tp = S.op("dve", lambda e: e.tensor_mul(hblk[:, :], hblk[:, :], Cbc[:, :]),
                      htoks + [tcx, bcC[-1]])
            tr = S.op("dve", lambda e: e.tensor_add(
                hblk[:, 0:8 * LC], hblk[:, 0:8 * LC], hblk[:, 8 * LC:16 * LC]), [tp])
            tr = S.op("dve", lambda e: e.tensor_add(
                hblk[:, 0:4 * LC], hblk[:, 0:4 * LC], hblk[:, 4 * LC:8 * LC]), [tr])
            tr = S.op("dve", lambda e: e.tensor_add(
                hblk[:, 0:2 * LC], hblk[:, 0:2 * LC], hblk[:, 2 * LC:4 * LC]), [tr])
            tr = S.op("dve", lambda e: e.tensor_add(
                ycol[:, :], hblk[:, 0:LC], hblk[:, LC:2 * LC]), [tr])
            ty2 = S.op("dve", lambda e, d=d, xpb_=xpb_: e.scalar_tensor_tensor(
                ycol[:, :], xpb_, bcol("Dp", d), ycol[:, :],
                ALU.mult, ALU.add), [tr, txp])
            ty3 = S.op("dve", lambda e, d=d: e.tensor_mul(
                y3slice(d), ycol[:, :], rld[:, 2 * LC:3 * LC]), [ty2, tzs] + op_toks[-1:])
            y3_toks.append(ty3)
            rd_guard[par] = [ty3]
        for m in range(NDT):
            wsl = wstat[:, (m % 2) * 2048:(m % 2) * 2048 + 2048]
            twl = S.dma("pool", wsl, woutQ[:, m * 2048:(m + 1) * 2048],
                        op_toks[-4:-2] if len(op_toks) >= 4 else prev_chunk)
            pslc = (PA if m % 2 == 0 else PB)[:, 0:LC]
            mmt = []
            for k in range(NDSL):
                for nb in range(2):
                    mmt.append(S.op("te", lambda e, pslc=pslc, wsl=wsl, k=k, nb=nb: e.matmul(
                        pslc[:, nb * 512:(nb + 1) * 512], wsl[:, k * 128:(k + 1) * 128],
                        y3slice(k)[:, nb * 512:(nb + 1) * 512],
                        start=(k == 0), stop=(k == NDSL - 1)),
                        [twl] + y3_toks + (op_toks[-4:-3] if len(op_toks) >= 4 else [])))
            txn = S.dma("sp", evb[:, 0:LC], sp_xn[m * 128:(m + 1) * 128, c0:c0 + LC],
                        xn_spill + op_toks[-4:])
            tx1 = S.op("dve", lambda e, pslc=pslc: e.tensor_add(
                evb[:, 1024:1024 + LC], pslc, evb[:, 0:LC]), mmt + [txn])
            tspl = S.dma("sp", sp_x1[m * 128:(m + 1) * 128, c0:c0 + LC],
                         evb[:, 1024:1024 + LC], [tx1])
            op_toks += [tx1, tspl]
        prev_chunk = S.barrier_toks()

    S.phase("P4")
    x1_done = prev_chunk
    x2_sp = []
    wl_tok = [[], []]
    wl2_tok = [[], []]
    for ch in range(NCH):
        c0 = ch * LC
        x1_toks = []
        for m in range(NDT):
            tld = S.dma("sp", x1c[:, m * LC:(m + 1) * LC],
                        sp_x1[m * 128:(m + 1) * 128, c0:c0 + LC], x1_done + x2_sp[-2:])
            t16 = S.op("dve", lambda e, m=m: e.tensor_copy(
                x1bf[:, m * LC:(m + 1) * LC], x1c[:, m * LC:(m + 1) * LC]), [tld])
            x1_toks += [tld, t16]
        hf_toks = []
        for m in range(NFT):
            wsl = wstat[:, (m % 2) * 2048:(m % 2) * 2048 + NDT * 128]
            twl = S.dma("pool", wsl, w1Q[:, m * 1024:(m + 1) * 1024],
                        wl_tok[m % 2] + wl2_tok[m % 2] + x2_sp[-2:])
            pslc = (PA if m % 2 == 0 else PB)[:, 0:LC]
            mmt = []
            for k in range(NDT):
                for nb in range(2):
                    mmt.append(S.op("te", lambda e, pslc=pslc, wsl=wsl, k=k, nb=nb: e.matmul(
                        pslc[:, nb * 512:(nb + 1) * 512], wsl[:, k * 128:(k + 1) * 128],
                        x1bf[:, k * LC + nb * 512:k * LC + (nb + 1) * 512],
                        start=(k == 0), stop=(k == NDT - 1)),
                        [twl] + x1_toks + hf_toks[-2:-1]))
            wl_tok[m % 2] = [mmt[-1]]
            tev = S.op("act", lambda e, pslc=pslc, m=m: e.activation(
                hfb[:, m * LC:(m + 1) * LC], pslc, AF.Gelu, bias=bcol("b1", m)), mmt)
            hf_toks.append(tev)
        w2_toks = []
        for m in range(NDT):
            wsl = wstat[:, (m % 2) * 2048:(m % 2) * 2048 + 2048]
            msl = movb[:, (m % 2) * 2048:(m % 2) * 2048 + 2048]
            wdeps = wl2_tok[m % 2] + w2_toks[-2:] + wl_tok[0] + wl_tok[1]
            twl = S.dma("pool", wsl, w2Q[:, m * 4096:m * 4096 + 2048], wdeps)
            twl2 = S.dma("pool", msl, w2Q[:, m * 4096 + 2048:(m + 1) * 4096], wdeps)
            pslc = (PA if m % 2 == 0 else PB)[:, 0:LC]
            mmt = []
            for k in range(NFT):
                stat = wsl[:, k * 128:(k + 1) * 128] if k < 16 else \
                    msl[:, (k - 16) * 128:(k - 15) * 128]
                dep = twl if k < 16 else twl2
                for nb in range(2):
                    mmt.append(S.op("te", lambda e, pslc=pslc, stat=stat, k=k, nb=nb: e.matmul(
                        pslc[:, nb * 512:(nb + 1) * 512], stat,
                        hfb[:, k * LC + nb * 512:k * LC + (nb + 1) * 512],
                        start=(k == 0), stop=(k == NFT - 1)),
                        [dep] + hf_toks + (w2_toks[-4:-3] if len(w2_toks) >= 4 else [])))
            wl2_tok[m % 2] = [mmt[-1]]
            tx2 = S.op("dve", lambda e, pslc=pslc, m=m: e.scalar_tensor_tensor(
                evb[:, 0:LC], pslc, bcol("b2", m), x1c[:, m * LC:(m + 1) * LC],
                ALU.add, ALU.add), mmt + x1_toks)
            tspl = S.dma("sp", sp_x2[m * 128:(m + 1) * 128, c0:c0 + LC], evb[:, 0:LC],
                         [tx2])
            w2_toks += [tx2, tspl]
        x2_sp += w2_toks
        x1_done = S.barrier_toks()

    S.phase("P5")
    wo_toks = [S.dma("pool", woS[:, k * KB:(k + 1) * KB],
                     woT[k * 128:(k + 1) * 128, :], x1_done) for k in range(NDT)]
    par_guard = [[], []]
    p5b_tok = [[] for _ in range(4)]
    stat_guard = [[], []]
    prev_tffn = []
    prev_tsum = []
    fin = []
    for tt in range(NTOK):
        par = tt % 2
        r0 = tt * 128
        xb_t = xbt[par]
        tldx = S.dma("sp", xb_t[:, :], xb[r0:r0 + 128, :], par_guard[par])
        tln = ln_tile(xb_t[:, :], xnt5[par][:, :], 8 + 4 * par,
                      [tldx] + par_guard[par], scr=lnscr[par][:, :])
        p5banks = [PA[:, 1024:1024 + 128], PA[:, 1536:1536 + 128],
                   PB[:, 1024:1024 + 128], PB[:, 1536:1536 + 128]]
        st_toks = []
        for j in range(NDT):
            pslc = p5banks[j % 4]
            ttr = S.op("te", lambda e, p=pslc, j=j, par=par: e.transpose(
                p, xnt5[par][:, j * 128:(j + 1) * 128], ident[:, :]),
                [tln] + p5b_tok[j % 4])
            tev = S.op("act", lambda e, p=pslc, j=j, par=par: e.copy(
                xst5[:, par * KB + j * 128:par * KB + (j + 1) * 128], p),
                [ttr] + stat_guard[par])
            p5b_tok[j % 4] = [tev]
            st_toks.append(tev)
        mmt = []
        for k in range(NDT):
            for nb in range(2):
                mmt.append(S.op("te", lambda e, k=k, nb=nb, par=par: e.matmul(
                    PA[:, nb * 512:(nb + 1) * 512],
                    xst5[:, par * KB + k * 128:par * KB + (k + 1) * 128],
                    woS[:, k * KB + nb * 512:k * KB + (nb + 1) * 512],
                    start=(k == 0), stop=(k == NDT - 1)),
                    [st_toks[k]] + wo_toks + prev_tffn))
        stat_guard[par] = [mmt[-1]]
        tcu = S.dma("sp", cut[par][:, :], cc_out[r0:r0 + 128, :],
                    [tcoll] + par_guard[par])
        tffn = S.op("dve", lambda e, par=par: e.tensor_add(
            evb[:, par * 2048:par * 2048 + D], PA[:, 0:D], cut[par][:, :]),
            mmt + [tcu] + par_guard[par])
        prev_tffn = [tffn]
        ttanh = S.op("act", lambda e, par=par: e.activation(
            evb[:, par * 2048:par * 2048 + D], evb[:, par * 2048:par * 2048 + D],
            AF.Tanh), [tffn])
        tx2l = S.dma("sp", x2s[par][:, :].rearrange("p (j c) -> p j c", j=NDT),
                     sp_x2[:, r0:r0 + 128].rearrange("(j p) c -> p j c", p=128),
                     x2_sp + par_guard[par])
        tr2 = []
        for j in range(NDT):
            pslc = PB[:, j * 128:(j + 1) * 128]
            tr2.append(S.op("te", lambda e, p=pslc, j=j, par=par: e.transpose(
                p, x2s[par][:, j * 128:(j + 1) * 128], ident[:, :]),
                [tx2l] + prev_tsum))
        tsum = S.op("dve", lambda e, par=par: e.tensor_add(
            evb[:, par * 2048 + 1024:par * 2048 + 1024 + D], PB[:, 0:D],
            xbt[par][:, :]), tr2 + [tldx, tln] + par_guard[par])
        prev_tsum = [tsum]
        tout = S.op("dve", lambda e, par=par: e.tensor_add(
            evb[:, par * 2048 + 1024:par * 2048 + 1024 + D],
            evb[:, par * 2048 + 1024:par * 2048 + 1024 + D],
            evb[:, par * 2048:par * 2048 + D]), [tsum, ttanh])
        tst = S.dma("sp", out[r0:r0 + 128, :], evb[:, par * 2048 + 1024:par * 2048 + 1024 + D],
                    [tout])
        par_guard[par] = [tst, tout, mmt[-1], tr2[-1], tsum]
        fin = [tst]
    S.op("sp", lambda e: e.nop(), fin)

    S.emit(nc)
    return nc


def kernel(**inputs):
    f = np.float32
    x = np.ascontiguousarray(np.asarray(inputs["L0_attn_x"], f))

    def T(a):
        return np.ascontiguousarray(np.asarray(a, f).T)

    def col(a):
        return np.ascontiguousarray(np.asarray(a, f).reshape(-1, 1))

    def qform(wT, K, M):
        return np.ascontiguousarray(
            wT.reshape(K, 128, M, 128).transpose(1, 2, 0, 3).reshape(128, M * K * 128))

    import ml_dtypes
    onehot = np.zeros((32, 32 * 128), np.float32)
    for n in range(32):
        onehot[n, n * 128:(n + 1) * 128] = 1.0
    onehot = onehot.astype(ml_dtypes.bfloat16)

    common = {
        "winQ": qform(T(inputs["L0_attn_in_proj_w"]), NDT, 32),
        "cwc": np.ascontiguousarray(
            np.asarray(inputs["L0_attn_conv_w"], f).reshape(16, 128, 3)
            .transpose(1, 0, 2).reshape(128, 48)),
        "cbv": col(inputs["L0_attn_conv_b"]),
        "wxT": T(inputs["L0_attn_x_proj_w"]),
        "wdtT": T(inputs["L0_attn_dt_proj_w"]),
        "dtbv": col(inputs["L0_attn_dt_proj_b"]),
        "AsbH": np.ascontiguousarray(
            np.asarray(inputs["L0_attn_A_log"], f).reshape(16, 128, 16)
            .transpose(1, 0, 2).reshape(128, 256)),
        "Dpv": col(inputs["L0_attn_Dp"]),
        "woutQ": qform(T(inputs["L0_attn_out_proj_w"]), NDSL, NDT),
        "w1Q": qform(T(inputs["L0_attn_w1"]), NDT, NFT),
        "b1v": col(inputs["L0_attn_b1"]),
        "w2Q": qform(T(inputs["L0_attn_w2"]), NFT, NDT),
        "b2v": col(inputs["L0_attn_b2"]),
        "wfTl": T(inputs["L0_ffn_wf"]), "ufTl": T(inputs["L0_ffn_uf"]),
        "bfvl": col(inputs["L0_ffn_bf"]),
        "wiTl": T(inputs["L0_ffn_wi"]), "uiTl": T(inputs["L0_ffn_ui"]),
        "bivl": col(inputs["L0_ffn_bi"]),
        "wcTl": T(inputs["L0_ffn_wc"]), "ucTl": T(inputs["L0_ffn_uc"]),
        "bcvl": col(inputs["L0_ffn_bc"]),
        "woT": T(inputs["L0_ffn_wo"]), "uoT": T(inputs["L0_ffn_uo"]),
        "bov": col(inputs["L0_ffn_bo"]),
        "incv": col(inputs["L0_ffn_in_c"]),
        "ohb": onehot,
    }
    in_maps = []
    for b in range(NCORE):
        m = dict(common)
        m["xb"] = np.ascontiguousarray(x[b])
        m["xr"] = np.ascontiguousarray(x[:, b * RPC:(b + 1) * RPC, :].reshape(L, D))
        in_maps.append(m)

    nc = build_program()
    res = run_bass_kernel_spmd(nc, in_maps, core_ids=list(range(NCORE)))
    return np.stack([np.asarray(res.results[b]["out"], np.float32)
                     for b in range(NCORE)], axis=0)
